# version 46
# baseline (speedup 1.0000x reference)
import numpy as np

import concourse.bass as bass
import concourse.mybir as mybir
from concourse.bass_utils import run_bass_kernel_spmd

F32 = mybir.dt.float32
BF16 = mybir.dt.bfloat16
AF = mybir.ActivationFunctionType
OP = mybir.AluOpType

SLOT = 66


class CFG:
    def __init__(self, T=2048, CH=256, BPC=2, ncore=8):
        self.T = T
        self.CH = CH
        self.BPC = BPC
        self.ncore = ncore
        self.C = 1024
        self.H = 16
        self.DH = 64
        self.NT = T // 128
        self.NCH = T // CH
        self.TPC = CH // 128
        offs = [0]
        o = 1
        while o < T:
            offs.append(o)
            o *= 2
        self.offsets = offs
        self.big_offsets = [o for o in offs if o >= 256]
        self.NO = len(self.big_offsets)
        self.stage = 99


def fenwick_mask_np(T):
    m = np.zeros((T, T), dtype=bool)
    for i in range(T):
        j = i + 1
        while j <= T:
            m[i, j - 1] = True
            j += j & -j
        step = 1
        while i - step >= 0:
            m[i, i - step] = True
            step *= 2
    m &= np.tril(np.ones((T, T), dtype=bool))
    return m


def aux_inputs(cfg):
    big = fenwick_mask_np(384)
    m0 = big[128:256, 128:256]
    m1 = big[256:384, 128:256]
    m01 = np.concatenate([m0.T, m1.T], axis=1).astype(np.float32)
    mb0 = np.where(m0.T > 0.5, np.float32(0.0), np.float32(-1e30)).astype(np.float32)
    mb1 = np.where(m1.T > 0.5, np.float32(0.0), np.float32(-1e30)).astype(np.float32)
    mb01x = np.concatenate([mb0, mb1], axis=1)
    ident = np.eye(128, dtype=np.float32)
    return {"m01": m01, "mb01x": mb01x, "ident": ident}


class Prog:
    ENGS = ("tensor", "scalar", "vector", "gpsimd", "sync")
    DSEMS = ("dw", "dws", "dx0", "dx1", "dy0", "dy1")

    def __init__(self):
        self.q = {e: [] for e in self.ENGS}
        self.cnt = {k: 0 for k in ("t", "s", "v", "g", "y") + self.DSEMS}
        self.esem = {"tensor": "t", "scalar": "s", "vector": "v",
                     "gpsimd": "g", "sync": "y"}
        self.maxwait = {e: {} for e in self.ENGS}

    def emit(self, eng, fn, deps=(), dma=False, dsem=None):
        if dma:
            key = dsem
            self.cnt[key] += 16
            ticket = (key, self.cnt[key])
            inc = 16
        else:
            key = self.esem[eng]
            self.cnt[key] += 1
            ticket = (key, self.cnt[key])
            inc = 1
        waits = []
        mw = self.maxwait[eng]
        best = {}
        for d in deps:
            if d is None:
                continue
            sk, sv = d
            if sv > best.get(sk, 0):
                best[sk] = sv
        for sk, sv in best.items():
            if mw.get(sk, 0) >= sv:
                continue
            mw[sk] = sv
            waits.append((sk, sv))
        self.q[eng].append((fn, waits, key, inc))
        return ticket

    def wait_only(self, eng, deps):
        waits = []
        mw = self.maxwait[eng]
        for d in deps:
            if d is None:
                continue
            sk, sv = d
            if mw.get(sk, 0) >= sv:
                continue
            mw[sk] = sv
            waits.append((sk, sv))
        self.q[eng].append((None, waits, None, 0))

    def finalize(self):
        esems = set(self.esem.values())
        waited = {}
        for e in self.ENGS:
            for fn, waits, key, inc in self.q[e]:
                for sk, sv in waits:
                    if sk in esems:
                        waited.setdefault(sk, set()).add(sv)
        remap = {}
        for sk, vals in waited.items():
            sv = sorted(vals)
            remap[sk] = {v: i + 1 for i, v in enumerate(sv)}
        import bisect
        rank_upto = {}
        for sk, vals in waited.items():
            rank_upto[sk] = sorted(vals)
        for e in self.ENGS:
            nq = []
            cnt = {}
            for fn, waits, key, inc in self.q[e]:
                nw = []
                for sk, sv in waits:
                    if sk in esems:
                        lst = rank_upto.get(sk, [])
                        r = bisect.bisect_right(lst, sv)
                        if r > 0:
                            nw.append((sk, r))
                    else:
                        nw.append((sk, sv))
                if key is not None and key in esems:
                    c = cnt.get(key, 0) + 1
                    cnt[key] = c
                    keep = c in waited.get(key, ())
                    nq.append((fn, nw, key if keep else None, inc))
                else:
                    nq.append((fn, nw, key, inc))
            self.q[e] = nq

    def replay(self, eng, ec, sems):
        for fn, waits, key, inc in self.q[eng]:
            for (sk, sv) in waits:
                ec.wait_ge(sems[sk], sv)
            if fn is not None:
                instr = fn(ec)
                if key is not None:
                    instr.then_inc(sems[key], inc)


def mx(*tix):
    best = {}
    for t in tix:
        if t is None:
            continue
        k, v = t
        if v > best.get(k, 0):
            best[k] = v
    return [(k, v) for k, v in best.items()]


def build_program(cfg):
    nc = bass.Bass("TRN2", target_bir_lowering=False)
    T, CH, BPC, C, H = cfg.T, cfg.CH, cfg.BPC, cfg.C, cfg.H
    NT, NCH, TPC = cfg.NT, cfg.NCH, cfg.TPC
    NDC = C // 128
    NCC = 2 * NDC
    BIG = cfg.big_offsets
    NO = max(cfg.NO, 1)
    VA_W = H * SLOT

    d_x = nc.dram_tensor("x", [BPC, T, C], F32, kind="ExternalInput").ap()
    d_wqkv = nc.dram_tensor("wqkv", [C, 3 * C], F32, kind="ExternalInput").ap()
    d_bqkv = nc.dram_tensor("bqkv", [3 * C], F32, kind="ExternalInput").ap()
    d_bqkv_col = nc.dram_tensor("bqkv_col", [128, NCC], F32, kind="ExternalInput").ap()
    d_wproj = nc.dram_tensor("wproj", [C, C], F32, kind="ExternalInput").ap()
    d_bproj = nc.dram_tensor("bproj", [C], F32, kind="ExternalInput").ap()
    d_win = nc.dram_tensor("win", [C, 64], F32, kind="ExternalInput").ap()
    d_bin = nc.dram_tensor("bin", [64], F32, kind="ExternalInput").ap()
    d_w1dup = nc.dram_tensor("w1dup", [128, 16], F32, kind="ExternalInput").ap()
    d_b1col = nc.dram_tensor("b1col", [128, 8], F32, kind="ExternalInput").ap()
    d_w2dup = nc.dram_tensor("w2dup", [128, 16], F32, kind="ExternalInput").ap()
    d_b2 = nc.dram_tensor("b2", [64], F32, kind="ExternalInput").ap()
    d_wout = nc.dram_tensor("wout", [64, C], F32, kind="ExternalInput").ap()
    d_bout = nc.dram_tensor("bout", [C], F32, kind="ExternalInput").ap()
    d_m01 = nc.dram_tensor("m01", [128, 256], F32, kind="ExternalInput").ap()
    d_mb01 = nc.dram_tensor("mb01x", [128, 256], F32, kind="ExternalInput").ap()
    d_ident = nc.dram_tensor("ident", [128, 128], F32, kind="ExternalInput").ap()
    d_y = nc.dram_tensor("y", [BPC, T, C], F32, kind="ExternalOutput").ap()

    import contextlib
    st = contextlib.ExitStack()
    sbt = lambda n, shp, dt: st.enter_context(nc.sbuf_tensor(n, shp, dt))
    pst = lambda n, shp, dt: st.enter_context(nc.psum_tensor(n, shp, dt))

    with st:
        wqkv = sbt("s_wqkv", [128, NDC * 3 * C], BF16)
        bqkv_col = sbt("s_bqkvcol", [128, NCC], F32)
        bv_row = sbt("s_bvrow", [32, C], BF16)
        wproj = sbt("s_wproj", [128, NDC * C], BF16)
        bproj_row = sbt("s_bprojrow", [32, C], BF16)
        win = sbt("s_win", [128, NDC * 64], BF16)
        bin_col = sbt("s_bincol", [64, 1], F32)
        w1dup = sbt("s_w1dup", [128, 16], F32)
        b1col = sbt("s_b1col", [128, 8], F32)
        w2dup = sbt("s_w2dup", [128, 16], F32)
        b2col = sbt("s_b2col", [64, 1], F32)
        w1exp = sbt("s_w1exp", [64, 8 * 128], BF16)
        w2exp = sbt("s_w2exp", [128, 8 * 64], BF16)
        woutaug = sbt("s_woutaug", [65, C], BF16)
        mb01 = sbt("s_mb01", [128, 256], BF16)
        ident = sbt("s_ident", [128, 128], BF16)
        identf = sbt("s_identf", [128, 128], F32)
        ones32 = sbt("s_ones32", [32, 128], BF16)
        bo = sbt("s_bo", [128, NDC * 16], BF16)
        cst = sbt("s_cst", [128, 3], F32)

        kT = sbt("s_kT", [128, NDC * T], BF16)
        qZ = sbt("s_qZ", [128, 2 * NDC * CH], BF16)
        vaug = sbt("s_vaug", [128, NT * VA_W], BF16)
        xbuf = sbt("s_x", [128, 2 * TPC * C], F32)
        xln = sbt("s_xln", [128, 2 * TPC * C], F32)
        xT = sbt("s_xT", [128, 2 * NDC * CH], BF16)
        pmask = sbt("s_pmask", [128, 4 * 512], BF16)
        scb = sbt("s_scb", [16, NO * CH], F32)
        sct = sbt("s_sct", [128, TPC * NO * 16], BF16)
        tmpsp = sbt("s_tmpsp", [128, 2 * VA_W], BF16)
        ebuf = sbt("s_ebuf", [128, 2 * CH], BF16)
        oT = sbt("s_oT", [128, NDC * CH], BF16)
        sTb = sbt("s_sTb", [64, CH], BF16)
        gaugT = sbt("s_gaugT", [65, CH], BF16)
        hT = sbt("s_hT", [128, 2 * CH], BF16)
        stats = sbt("s_stats", [128, 16 * TPC], F32)
        recip16 = sbt("s_recip", [128, 16], F32)

        pbank = [pst("p_A", [128, 512], F32), pst("p_B", [128, 512], F32)]
        psc = [pst("p_sc0", [128, 512], F32), pst("p_sc1", [128, 512], F32),
               pst("p_sc2", [128, 512], F32)]
        po = [pst("p_o0", [128, 512], F32), pst("p_o1", [128, 512], F32),
              pst("p_o2", [128, 512], F32)]

        sems = {}
        for name in ("t", "s", "v", "g", "y") + Prog.DSEMS:
            sems[name] = st.enter_context(nc.semaphore("sem_" + name))

        nc.const_aps.register(F32, -8.0, cst[:, 0:1])
        nc.const_aps.register(F32, 1e-5, cst[:, 1:2])
        nc.const_aps.register(F32, 0.0, cst[:, 2:3])

        P = Prog()

        t_cexp = P.emit("vector", lambda ec: ec.memset(cst[:, 0:1], -8.0))
        t_ceps = P.emit("vector", lambda ec: ec.memset(cst[:, 1:2], 1e-5))
        t_czer = P.emit("vector", lambda ec: ec.memset(cst[:, 2:3], 0.0))
        t_cst = [t_cexp, t_ceps, t_czer]
        t_o32a = P.emit("vector", lambda ec: ec.memset(ones32[:, :], 0.0))
        t_ones = P.emit("vector", lambda ec: ec.memset(ones32[0:1, :], 1.0), deps=[t_o32a])
        t_qz0 = P.emit("vector", lambda ec: ec.memset(qZ[:, :], 0.0))
        t_bvz = P.emit("vector", lambda ec: ec.memset(bv_row[:, :], 0.0))
        t_bpz = P.emit("vector", lambda ec: ec.memset(bproj_row[:, :], 0.0))
        t_bo0 = P.emit("vector", lambda ec: ec.memset(bo[:, :], 0.0))
        t_bo = t_bo0
        for dc in range(NDC):
            for k in range(2):
                cidx = dc * 16 + 2 * dc + k
                ap = bo[64 * k:64 * k + 64, cidx:cidx + 1]
                t_bo = P.emit("vector", lambda ec, ap=ap: ec.memset(ap, 1.0),
                              deps=[t_bo0])
        t_va0 = P.emit("vector", lambda ec: ec.memset(vaug[:, :], 0.0))
        ones_ap = bass.AP(vaug, 0, [[VA_W * NT, 128], [VA_W, NT], [SLOT, H], [1, 1]])
        t_va1 = P.emit("vector", lambda ec: ec.memset(ones_ap, 1.0), deps=[t_va0])
        t_gaug1 = P.emit("vector", lambda ec: ec.memset(gaugT[64:65, :], 1.0))

        def cdma(dst_ap, src_ap):
            return P.emit("gpsimd", lambda ec: ec.dma_start(dst_ap, src_ap),
                          dma=True, dsem="dw")

        t_wqkv = cdma(wqkv[:, :], d_wqkv.rearrange("(n p) c -> p n c", p=128))
        t_wproj = cdma(wproj[:, :], d_wproj.rearrange("(n p) c -> p n c", p=128))
        t_win = cdma(win[:, :], d_win.rearrange("(n p) c -> p n c", p=128))
        t_bv = P.emit("gpsimd", lambda ec: ec.dma_start(
            bv_row[0:1, :], d_bqkv[2 * C:3 * C].unsqueeze(0)),
            deps=[t_bvz], dma=True, dsem="dw")
        t_bproj = P.emit("gpsimd", lambda ec: ec.dma_start(
            bproj_row[0:1, :], d_bproj.unsqueeze(0)),
            deps=[t_bpz], dma=True, dsem="dw")
        t_wo0 = cdma(woutaug[0:64, :], d_wout)
        t_wo1 = cdma(woutaug[64:65, :], d_bout.unsqueeze(0))
        t_m01 = cdma(mb01[:, :], d_mb01)
        t_ident = cdma(ident[:, :], d_ident)

        t_identf = P.emit("sync", lambda ec: ec.dma_start(identf[:, :], d_ident), dma=True, dsem="dws")
        t_bqc = P.emit("sync", lambda ec: ec.dma_start(bqkv_col[:, :], d_bqkv_col), dma=True, dsem="dws")
        t_binc = P.emit("sync", lambda ec: ec.dma_start(bin_col[:, :], d_bin.unsqueeze(1)), dma=True, dsem="dws")
        t_w1d = P.emit("sync", lambda ec: ec.dma_start(w1dup[:, :], d_w1dup), dma=True, dsem="dws")
        t_b1c = P.emit("sync", lambda ec: ec.dma_start(b1col[:, :], d_b1col), dma=True, dsem="dws")
        t_w2d = P.emit("sync", lambda ec: ec.dma_start(w2dup[:, :], d_w2dup), dma=True, dsem="dws")
        t_b2c = P.emit("sync", lambda ec: ec.dma_start(b2col[:, :], d_b2.unsqueeze(1)), dma=True, dsem="dws")

        t_dw_all = ("dw", P.cnt["dw"])
        t_dws_all = ("dws", P.cnt["dws"])
        t_wqkv = t_wproj = t_win = t_bv = t_bproj = t_wo0 = t_wo1 = t_m01 = t_ident = t_dw_all
        t_identf = t_bqc = t_binc = t_w1d = t_b1c = t_w2d = t_b2c = t_dws_all

        t_w12 = []
        for ic in range(8):
            for k in range(2):
                i = 2 * ic + k
                dst1 = w1exp[0:64, ic * 128 + 64 * k: ic * 128 + 64 * k + 64]
                s1 = w1dup[0:64, i:i + 1]
                t_w12.append(P.emit(
                    "vector", lambda ec, dst1=dst1, s1=s1: ec.tensor_scalar_mul(
                        dst1, ident[0:64, 0:64], s1),
                    deps=[t_ident, t_w1d]))
                dst2 = w2exp[64 * k:64 * k + 64, ic * 64: ic * 64 + 64]
                src2 = ident[64 * k:64 * k + 64, 64 * k:64 * k + 64]
                s2 = w2dup[64 * k:64 * k + 64, i:i + 1]
                t_w12.append(P.emit(
                    "vector", lambda ec, dst2=dst2, src2=src2, s2=s2: ec.tensor_scalar_mul(
                        dst2, src2, s2),
                    deps=[t_ident, t_w2d]))

        rot = [None, None, None, None, None]
        psc_readers = [[] for _ in range(8)]
        psc_i = [0]

        SCORE_POOL = (0, 1, 2, 3)

        def psc_next():
            bk = SCORE_POOL[psc_i[0] % len(SCORE_POOL)]
            psc_i[0] += 1
            return bk

        ROT6 = (2, 3, 4, 5, 6, 7)
        r6_i = [0]

        def rot6_next():
            bk = ROT6[r6_i[0] % 6]
            r6_i[0] += 1
            return bk

        pb_i = [0]
        QKV_POOL = (3, 4, 5, 6, 7)
        qkv_i = [0]

        def pb_next():
            bk = pb_i[0] % 2
            pb_i[0] += 1
            return bk

        def qkv_next():
            bk = QKV_POOL[qkv_i[0] % len(QKV_POOL)]
            qkv_i[0] += 1
            return bk

        rotb = psc + pbank + po

        def pbr_get(bi):
            return psc_readers[3 + bi]

        def pbr_set(bi, lst):
            psc_readers[3 + bi] = lst

        def rrd_get(bk):
            return psc_readers[bk]

        def rrd_set(bk, lst):
            psc_readers[bk] = lst

        kv_rd = [[] for _ in range(NT)]
        qT_rd = []
        xT_rd = [[], []]
        xln_rd = [[], []]
        xln_w = [[], []]
        ebuf_rd = [[], [], []]
        scr_rd = []
        tt1_rd = [[]]
        hT_rd = [[], []]
        pmask_rd = [[], [], [], []]
        praw_rd = [[], []]
        x_ring_free = [[], []]
        oT_rd = []
        sTb_rd = []
        gaug_rd = []
        scb_rd = []
        sct_rd = []
        osb_rd = []
        stats_rd = [[] for _ in range(2 * TPC)]
        tev_alt = [0]

        def po_slot2(h):
            if h < 7:
                return 0, h * SLOT
            if h < 14:
                return 1, (h - 7) * SLOT
            return 2, (h - 14) * SLOT

        def xsl(c, i):
            rr = c % 2
            return xbuf[:, (rr * TPC + i) * C:(rr * TPC + i + 1) * C]

        def emit_ln(c, slot, pre_deps_tile, sumx_mode):
            ts_sx, ts_sq, ts_m2, ts_var, ts_rs = [], [], [], [], []
            for i in range(TPC):
                sc0 = 16 * i + 8 * slot
                sb_i = 2 * i + slot
                xap = xsl(c, i)
                xlap = xln[:, (slot * TPC + i) * C:(slot * TPC + i + 1) * C]
                pre = pre_deps_tile[i]
                if sumx_mode == "reduce":
                    t_sx = P.emit("vector", lambda ec, xap=xap, sc0=sc0: ec.reduce_sum(
                        stats[:, sc0:sc0 + 1], xap, mybir.AxisListType.X),
                        deps=pre + stats_rd[sb_i])
                else:
                    t_sx = P.emit("vector", lambda ec, sc0=sc0: ec.tensor_add(
                        stats[:, sc0:sc0 + 1], stats[:, sc0:sc0 + 1],
                        stats[:, sc0 + 1:sc0 + 2]), deps=pre + stats_rd[sb_i])
                t_sq = P.emit("scalar", lambda ec, xap=xap, xlap=xlap, sc0=sc0: ec.activation(
                    xlap, xap, AF.Square, accum_out=stats[:, sc0 + 1:sc0 + 2]),
                    deps=pre + xln_rd[slot] + xln_w[slot] + [t_sx] + t_cst)
                t_m2 = P.emit("vector", lambda ec, sc0=sc0: ec.scalar_tensor_tensor(
                    stats[:, sc0 + 4:sc0 + 5], stats[:, sc0:sc0 + 1], 1.0 / (C * C),
                    stats[:, sc0:sc0 + 1], OP.mult, OP.mult), deps=[t_sx])
                ts_sx.append(t_sx)
                ts_sq.append(t_sq)
                ts_m2.append(t_m2)
            for i in range(TPC):
                sc0 = 16 * i + 8 * slot
                ts_var.append(P.emit("vector", lambda ec, sc0=sc0: ec.scalar_tensor_tensor(
                    stats[:, sc0 + 5:sc0 + 6], stats[:, sc0 + 1:sc0 + 2], 1.0 / C,
                    stats[:, sc0 + 4:sc0 + 5], OP.mult, OP.subtract),
                    deps=[ts_sq[i], ts_m2[i]]))
            for i in range(TPC):
                sc0 = 16 * i + 8 * slot
                t_ln = P.emit("scalar", lambda ec, sc0=sc0: ec.activation(
                    stats[:, sc0 + 6:sc0 + 7], stats[:, sc0 + 5:sc0 + 6],
                    AF.Ln, bias=1e-5), deps=[ts_var[i]] + t_cst)
                ts_rs.append(P.emit("scalar", lambda ec, sc0=sc0: ec.activation(
                    stats[:, sc0 + 7:sc0 + 8], stats[:, sc0 + 6:sc0 + 7],
                    AF.Exp, scale=-0.5), deps=[t_ln]))
            t_norm = []
            for i in range(TPC):
                sc0 = 16 * i + 8 * slot
                sb_i = 2 * i + slot
                xap = xsl(c, i)
                xlap = xln[:, (slot * TPC + i) * C:(slot * TPC + i + 1) * C]
                t_mr = P.emit("vector", lambda ec, sc0=sc0: ec.scalar_tensor_tensor(
                    stats[:, sc0 + 4:sc0 + 5], stats[:, sc0:sc0 + 1], 1.0 / C,
                    stats[:, sc0 + 7:sc0 + 8], OP.mult, OP.mult),
                    deps=[ts_rs[i], ts_var[i]])
                t_nm = P.emit("vector", lambda ec, xap=xap, xlap=xlap, sc0=sc0: ec.tensor_scalar(
                    xlap, xap, stats[:, sc0 + 7:sc0 + 8], stats[:, sc0 + 4:sc0 + 5],
                    OP.mult, OP.subtract), deps=[t_mr, ts_sq[i]])
                t_norm.append(t_nm)
                stats_rd[sb_i] = [t_nm]
            return t_norm

        def emit_transpose(src_t, src_base, dst_t, dst_base, t_dep_tiles, slot_rd,
                           tiles=None):
            if tiles is None:
                tiles = list(range(TPC))
            evs = []
            last_tr = None
            for dc in range(NDC):
                bank = rot6_next()
                dst = rotb[bank]
                war = psc_readers[bank]
                trs = []
                for si, i in enumerate(tiles):
                    src_ap = src_t[:, src_base + i * C + dc * 128: src_base + i * C + (dc + 1) * 128]
                    t_tr = P.emit("tensor", lambda ec, dst=dst, si=si, src_ap=src_ap: ec.transpose(
                        dst[:, si * 128:(si + 1) * 128], src_ap, identf[:, :]),
                        deps=([t_dep_tiles[i], t_identf] + war) if si == 0 else [t_dep_tiles[i]])
                    trs.append(t_tr)
                    last_tr = t_tr
                w = len(tiles) * 128
                out_ap = dst_t[:, dst_base + dc * CH + tiles[0] * 128:
                               dst_base + dc * CH + tiles[0] * 128 + w]
                t_ev = P.emit("vector", lambda ec, dst=dst, w=w, out_ap=out_ap: ec.tensor_copy(
                    out_ap, dst[:, 0:w]), deps=trs + slot_rd)
                psc_readers[bank] = [t_ev]
                evs.append(t_ev)
            return evs, last_tr

        def emit_ln1_stats(b, c):
            t_xl = xloads[(b, c)]
            ts_sq, ts_m2, ts_var, ts_rs = [], [], [], []
            for i in range(TPC):
                sc0 = 16 * i
                sb_i = 2 * i
                xap = xsl(c, i)
                xlap = xln[:, i * C:(i + 1) * C]
                t_sx = P.emit("scalar", lambda ec, xap=xap, xlap=xlap, sc0=sc0: ec.activation(
                    xlap, xap, AF.Identity, accum_out=stats[:, sc0:sc0 + 1]),
                    deps=[t_xl] + stats_rd[sb_i] + xln_rd[0] + xln_w[0] + t_cst)
                t_sq = P.emit("scalar", lambda ec, xap=xap, xlap=xlap, sc0=sc0: ec.activation(
                    xlap, xap, AF.Square, accum_out=stats[:, sc0 + 1:sc0 + 2]),
                    deps=[t_sx])
                t_m2 = P.emit("vector", lambda ec, sc0=sc0: ec.scalar_tensor_tensor(
                    stats[:, sc0 + 4:sc0 + 5], stats[:, sc0:sc0 + 1], 1.0 / (C * C),
                    stats[:, sc0:sc0 + 1], OP.mult, OP.mult), deps=[t_sx])
                ts_sq.append(t_sq)
                ts_m2.append(t_m2)
            for i in range(TPC):
                sc0 = 16 * i
                ts_var.append(P.emit("vector", lambda ec, sc0=sc0: ec.scalar_tensor_tensor(
                    stats[:, sc0 + 5:sc0 + 6], stats[:, sc0 + 1:sc0 + 2], 1.0 / C,
                    stats[:, sc0 + 4:sc0 + 5], OP.mult, OP.subtract),
                    deps=[ts_sq[i], ts_m2[i]]))
            for i in range(TPC):
                sc0 = 16 * i
                t_ln = P.emit("scalar", lambda ec, sc0=sc0: ec.activation(
                    stats[:, sc0 + 6:sc0 + 7], stats[:, sc0 + 5:sc0 + 6],
                    AF.Ln, bias=1e-5), deps=[ts_var[i]] + t_cst)
                ts_rs.append(P.emit("scalar", lambda ec, sc0=sc0: ec.activation(
                    stats[:, sc0 + 7:sc0 + 8], stats[:, sc0 + 6:sc0 + 7],
                    AF.Exp, scale=-0.5), deps=[t_ln]))
            t_norm = []
            for i in range(TPC):
                sc0 = 16 * i
                sb_i = 2 * i
                xap = xsl(c, i)
                xlap = xln[:, i * C:(i + 1) * C]
                t_mr = P.emit("vector", lambda ec, sc0=sc0: ec.scalar_tensor_tensor(
                    stats[:, sc0 + 4:sc0 + 5], stats[:, sc0:sc0 + 1], 1.0 / C,
                    stats[:, sc0 + 7:sc0 + 8], OP.mult, OP.mult),
                    deps=[ts_rs[i], ts_var[i]])
                t_nm = P.emit("vector", lambda ec, xap=xap, xlap=xlap, sc0=sc0: ec.tensor_scalar(
                    xlap, xap, stats[:, sc0 + 7:sc0 + 8], stats[:, sc0 + 4:sc0 + 5],
                    OP.mult, OP.subtract), deps=[t_mr, ts_sq[i]])
                t_norm.append(t_nm)
                stats_rd[sb_i] = [t_nm]
            xln_w[0] = [t_norm[-1]]
            return t_norm

        def emit_ln1_tr(t_norm1):
            evs, last_tr = emit_transpose(xln, 0, xT, 0, t_norm1, xT_rd[0])
            xln_rd[0] = [last_tr]
            xT_rd[0] = []
            return evs

        def emit_ln1_block(b, c):
            t_norm1 = emit_ln1_stats(b, c)
            return t_norm1, emit_ln1_tr(t_norm1)

        SPB = (5, 6)

        def build_sparse(c2, qz_done2, qkv_done2):
            t0s = c2 * CH
            tile0s = t0s // 128
            sct_ready2 = {}
            steps = []
            valid_big = [(oi, o) for oi, o in enumerate(BIG) if t0s + CH > o]
            if not valid_big:
                return steps, None, sct_ready2
            per_bank = max(512 // CH, 1)
            nbank = (len(valid_big) + per_bank - 1) // per_bank
            sc_banks = [SPB[k] for k in range(nbank)]
            st_cell = {"war": None, "final": None}

            def get_war():
                if st_cell["war"] is None:
                    w = []
                    for bk in sc_banks:
                        w += psc_readers[bk]
                    st_cell["war"] = w
                return st_cell["war"]

            ei_cell = [0]
            for vi, (oi, o) in enumerate(valid_big):
                scp = rotb[sc_banks[vi // per_bank]]
                col0 = (vi % per_bank) * CH
                a = max(0, o - t0s)
                w = CH - a
                for dc in range(NDC):
                    def step(vi=vi, oi=oi, o=o, dc=dc, scp=scp, col0=col0, a=a, w=w):
                        es = ei_cell[0] % 3
                        ei_cell[0] += 1
                        kap_lo = kT[0:64, dc * T + t0s + a - o: dc * T + t0s + a - o + w]
                        kap_hi = kT[64:128, dc * T + t0s + a - o: dc * T + t0s + a - o + w]
                        qlo = qZ[0:64, dc * CH + a: dc * CH + a + w]
                        qhi = qZ[64:128, NDC * CH + dc * CH + a: NDC * CH + dc * CH + a + w]
                        if es != 1:
                            e0 = (es // 2) * CH
                            elo = ebuf[0:64, e0 + a:e0 + a + w]
                            ehi = ebuf[64:128, e0 + a:e0 + a + w]
                            e_in = ebuf[:, e0 + a:e0 + a + w]
                        else:
                            elo = tmpsp[0:64, VA_W + a: VA_W + a + w]
                            ehi = tmpsp[64:128, VA_W + a: VA_W + a + w]
                            e_in = tmpsp[:, VA_W + a: VA_W + a + w]
                        dq = qz_done2.get(dc, qkv_done2)
                        lo_eng = "gpsimd" if dc % 2 == 0 else "vector"
                        t_e1 = P.emit(lo_eng, lambda ec, elo=elo, qlo=qlo, kap_lo=kap_lo: ec.tensor_mul(
                            elo, qlo, kap_lo), deps=dq + ebuf_rd[es] + (scr_rd if es == 1 else []))
                        t_e2 = P.emit("vector", lambda ec, ehi=ehi, qhi=qhi, kap_hi=kap_hi: ec.tensor_mul(
                            ehi, qhi, kap_hi), deps=dq + ebuf_rd[es] + (scr_rd if es == 1 else []))
                        boap = bo[:, dc * 16:(dc + 1) * 16]
                        out_ap = scp[0:16, col0 + a: col0 + a + w]
                        t_mm = P.emit("tensor", lambda ec, out_ap=out_ap, boap=boap, e_in=e_in, dc=dc: ec.matmul(
                            out_ap, boap, e_in, start=(dc == 0), stop=(dc == NDC - 1)),
                            deps=([t_e1, t_e2, t_bo] + get_war()) if dc == 0 else [t_e1, t_e2])
                        ebuf_rd[es] = [t_mm]
                        st_cell["final"] = t_mm
                        for jt in range((t0s + a - o) // 128, (t0s + a - o + w + 127) // 128):
                            kv_rd[jt].append(t_e1)
                            kv_rd[jt].append(t_e2)
                        qT_rd.append(t_e1)
                        qT_rd.append(t_e2)
                    steps.append(step)

            def finish():
                nonlocal scb_rd, sct_rd
                t_exp_by_oi = {}
                for vi, (oi, o) in enumerate(valid_big):
                    scp = rotb[sc_banks[vi // per_bank]]
                    col0 = (vi % per_bank) * CH
                    a = max(0, o - t0s)
                    w = CH - a
                    in_ap = scp[0:16, col0 + a: col0 + a + w]
                    out_ap = scb[0:16, oi * CH + a: oi * CH + a + w]
                    t_exp = P.emit("scalar", lambda ec, in_ap=in_ap, out_ap=out_ap: ec.activation(
                        out_ap, in_ap, AF.Exp, bias=-8.0, scale=0.125),
                        deps=[st_cell["final"]] + t_cst + scb_rd)
                    t_exp_by_oi[oi] = t_exp
                for bk in sc_banks:
                    psc_readers[bk] = list(t_exp_by_oi.values())
                for i in range(TPC):
                    qg = tile0s + i
                    for oi, o in valid_big:
                        if qg * 128 < o:
                            continue
                        bank = rot6_next()
                        dst = rotb[bank]
                        src_ap = scb[0:16, oi * CH + i * 128: oi * CH + (i + 1) * 128]
                        out_ap = sct[:, (i * NO + oi) * 16:(i * NO + oi) * 16 + 16]
                        idap = identf[0:16, 0:16]
                        t_tr = P.emit("tensor", lambda ec, dst=dst, src_ap=src_ap, idap=idap: ec.transpose(
                            dst[:, 0:16], src_ap, idap),
                            deps=[t_exp_by_oi[oi], t_identf] + psc_readers[bank])
                        scb_rd = [t_tr]
                        t_ev = P.emit("vector", lambda ec, dst=dst, out_ap=out_ap: ec.tensor_copy(
                            out_ap, dst[:, 0:16]), deps=[t_tr] + sct_rd)
                        psc_readers[bank] = [t_ev]
                        sct_ready2[(i, oi)] = t_ev

            return steps, finish, sct_ready2

        sched = [(b, c) for b in range(BPC) for c in range(NCH)]
        xloads = {}

        def emit_xload(b, c):
            rr = c % 2
            srcx = d_x[b, c * CH:(c + 1) * CH, :].rearrange("(n p) c -> p n c", p=128)
            dst_x = xbuf[:, rr * TPC * C:(rr + 1) * TPC * C]
            t = P.emit("sync", lambda ec, dst_x=dst_x, srcx=srcx: ec.dma_start(dst_x, srcx),
                       deps=x_ring_free[rr], dma=True, dsem="dx%d" % rr)
            x_ring_free[rr] = []
            xloads[(b, c)] = t

        ln1_cache = None
        qkv_cache = None
        sparse_cache = None
        xT1_ready = [[]]
        for idx, (b, c) in enumerate(sched):
            r = c % 2
            t0 = c * CH
            tile0 = t0 // 128
            if idx == 0:
                emit_xload(b, c)
            if idx + 1 < len(sched):
                emit_xload(*sched[idx + 1])

            if ln1_cache is not None:
                t_norm1, xT_ready = ln1_cache
                xT1_ready[0] = xT_ready

            def emit_qkv_qk(b_q, c_q):
                nonlocal qT_rd
                t0 = c_q * CH
                tile0 = t0 // 128
                war_kv = []
                for i in range(TPC):
                    war_kv += kv_rd[tile0 + i]
                qkv_evac = []
                qz_done = {}
                v_w = []
                vw_tile = {}
                for cc in range(NCC):
                    bi = qkv_next()
                    pb = rotb[bi]
                    t_mm = None
                    for dc in range(NDC):
                        lhs = wqkv[:, dc * 3 * C + cc * 128: dc * 3 * C + (cc + 1) * 128]
                        rhs = xT[:, dc * CH: (dc + 1) * CH]
                        xtd = [xT1_ready[0][dc]] if len(xT1_ready[0]) == NDC else xT1_ready[0]
                        t_mm = P.emit("tensor", lambda ec, pb=pb, lhs=lhs, rhs=rhs, dc=dc: ec.matmul(
                            pb[:, 0:CH], lhs, rhs, start=(dc == 0), stop=(dc == NDC - 1)),
                            deps=(rrd_get(bi) + xtd + [t_wqkv]) if dc == 0 else xtd)
                    if cc < NDC:
                        dlo = qZ[0:64, cc * CH:(cc + 1) * CH]
                        dhi = qZ[64:128, NDC * CH + cc * CH: NDC * CH + (cc + 1) * CH]
                        blo = bqkv_col[0:64, cc:cc + 1]
                        bhi = bqkv_col[64:128, cc:cc + 1]
                        t_ev1 = P.emit("scalar", lambda ec, dlo=dlo, blo=blo, pb=pb: ec.activation(
                            dlo, pb[0:64, 0:CH], AF.Identity, bias=blo, scale=1.0),
                            deps=[t_mm, t_bqc, t_qz0] + qT_rd)
                        t_ev = P.emit("vector", lambda ec, dhi=dhi, bhi=bhi, pb=pb: ec.tensor_scalar_add(
                            dhi, pb[64:128, 0:CH], bhi),
                            deps=[t_mm, t_bqc, t_qz0] + qT_rd)
                        rrd_set(bi, [t_ev1, t_ev])
                        qkv_evac.append(t_ev1)
                        qkv_evac.append(t_ev)
                        qz_done[cc] = [t_ev1, t_ev]
                        continue
                    ck = cc - NDC
                    dst = kT[:, ck * T + t0: ck * T + t0 + CH]
                    bcol = bqkv_col[:, cc:cc + 1]
                    if ck % 2 == 0:
                        t_ev = P.emit("scalar", lambda ec, dst=dst, bcol=bcol, pb=pb: ec.activation(
                            dst, pb[:, 0:CH], AF.Identity, bias=bcol, scale=1.0),
                            deps=[t_mm, t_bqc] + war_kv)
                    else:
                        t_ev = P.emit("vector", lambda ec, dst=dst, bcol=bcol, pb=pb: ec.tensor_scalar_add(
                            dst, pb[:, 0:CH], bcol),
                            deps=[t_mm, t_bqc] + war_kv)
                    rrd_set(bi, [t_ev])
                    qkv_evac.append(t_ev)
                qT_rd = []
                return dict(t0=t0, tile0=tile0, war_kv=war_kv, qkv_evac=qkv_evac, qz_done=qz_done, v_w=v_w, vw_tile=vw_tile)

            def emit_qkv_v(st_q):
                nonlocal qT_rd
                t0 = st_q["t0"]
                tile0 = st_q["tile0"]
                war_kv = st_q["war_kv"]
                qkv_evac = st_q["qkv_evac"]
                qz_done = st_q["qz_done"]
                v_w = st_q["v_w"]
                vw_tile = st_q["vw_tile"]
                for i in range(TPC):
                    jt = tile0 + i
                    for g in range(2):
                        bi = qkv_next()
                        pb = rotb[bi]
                        t_mm = None
                        for dc in range(NDC):
                            lhs = xT[:, dc * CH + i * 128: dc * CH + (i + 1) * 128]
                            rhs = wqkv[:, dc * 3 * C + 2 * C + g * 512: dc * 3 * C + 2 * C + (g + 1) * 512]
                            xtd = [xT1_ready[0][dc]] if len(xT1_ready[0]) == NDC else xT1_ready[0]
                            t_mm = P.emit("tensor", lambda ec, pb=pb, lhs=lhs, rhs=rhs, dc=dc: ec.matmul(
                                pb[:, 0:512], lhs, rhs, start=(dc == 0), stop=False),
                                deps=(rrd_get(bi) + xtd + [t_wqkv]) if dc == 0 else xtd)
                        rhsb = bv_row[0:32, g * 512:(g + 1) * 512]
                        t_mm = P.emit("tensor", lambda ec, pb=pb, rhsb=rhsb: ec.matmul(
                            pb[:, 0:512], ones32[0:32, :], rhsb, start=False, stop=True),
                            deps=[t_ones, t_bv])
                        h0 = g * 8
                        dst = bass.AP(vaug, jt * VA_W + h0 * SLOT + 1,
                                      [[NT * VA_W, 128], [SLOT, 8], [1, 64]])
                        srcp = bass.AP(pb, 0, [[512, 128], [64, 8], [1, 64]])
                        t_ev = P.emit("vector", lambda ec, dst=dst, srcp=srcp: ec.tensor_copy(dst, srcp),
                                      deps=[t_mm, t_va1] + war_kv)
                        rrd_set(bi, [t_ev])
                        v_w.append(t_ev)
                        vw_tile.setdefault(jt, []).append(t_ev)
                xT_rd[0] = [v_w[-1]]
                for i in range(TPC):
                    kv_rd[tile0 + i] = []

                

                return qkv_evac, qz_done, v_w, vw_tile, mx(*(qkv_evac + v_w))

            if qkv_cache is None:
                t_norm1, xT_ready = ln1_cache if ln1_cache else emit_ln1_block(b, c)
                xT1_ready[0] = xT_ready
                qkv_cache = emit_qkv_v(emit_qkv_qk(b, c))
            qkv_evac, qz_done, v_w, vw_tile, qkv_done = qkv_cache

            if sparse_cache is None:
                sp_steps0, sp_fin0, sct_ready = build_sparse(c, qz_done, qkv_done)
                for s_ in sp_steps0:
                    s_()
                if sp_fin0 is not None:
                    sp_fin0()
            else:
                sct_ready = sparse_cache

            if idx + 1 >= len(sched):
                ln1_stats_next = None
                ln1_cache = None

            osb_ready = []
            t_resid1 = [None] * (2 * TPC)
            oT_ready_all = []

            def emit_pv(job):
                t_exp_j, g_j, i_j, qg_j, deltas_j, nd_j, hp_j = job
                t_pv = None
                for hh in range(2):
                    h = 2 * hp_j + hh
                    bank2, scol = po_slot2(h)
                    for di, d in enumerate(deltas_j):
                        kb = qg_j - d
                        pcol = g_j * 512 + di * 256 + hh * 128
                        lhs = pmask[:, pcol:pcol + 128]
                        rhs = vaug[:, kb * VA_W + h * SLOT: kb * VA_W + h * SLOT + SLOT]
                        out_ap = po[bank2][:, scol:scol + SLOT]
                        st = not bank_started[bank2]
                        bank_started[bank2] = True
                        stp = ((not sparse_here) and (di == nd_j - 1)
                               and h in (6, 13, 15))
                        t_pv = P.emit("tensor", lambda ec, out_ap=out_ap, lhs=lhs, rhs=rhs, st=st, stp=stp: ec.matmul(
                            out_ap, lhs, rhs, start=st, stop=stp),
                            deps=([t_exp_j] + psc_readers[5 + bank2] + vw_tile.get(kb, [])) if st else [t_exp_j])
                        kv_rd[kb].append(t_pv)
                    pv_last_tile[bank2] = t_pv
                pmask_rd[g_j] = [t_pv]

            pending = []
            for i in range(TPC):
                qg = tile0 + i
                deltas = [d for d in (0, 1) if qg - d >= 0]
                nd = len(deltas)
                width2 = 2 * nd * 128
                pv_last_tile = [None, None, None]
                bank_started = [False, False, False]
                valid_o = [(oi, o) for oi, o in enumerate(BIG) if qg * 128 >= o]
                sparse_here = len(valid_o) > 0
                ss = 0

                t_ttA = None
                t_ttB = None
                if sparse_here:
                    for j, (oi, o) in enumerate(valid_o):
                        kb = qg - o // 128
                        prep = bass.AP(sct, (i * NO + oi) * 16,
                                       [[TPC * NO * 16, 128], [1, H], [0, SLOT]])
                        vin = vaug[:, kb * VA_W: (kb + 1) * VA_W]
                        if j == 0:
                            tout = tmpsp[:, 0:VA_W]
                            t_ttA = P.emit("gpsimd", lambda ec, tout=tout, vin=vin, prep=prep: ec.tensor_mul(
                                tout, vin, prep),
                                deps=[sct_ready[(i, oi)], t_va1] + tt1_rd[0])
                            t_last_o = t_ttA
                        elif j == 1:
                            tout = tmpsp[:, VA_W: 2 * VA_W]
                            t_ttB = P.emit("vector", lambda ec, tout=tout, vin=vin, prep=prep: ec.tensor_mul(
                                tout, vin, prep),
                                deps=[sct_ready[(i, oi)], t_va1] + scr_rd + ebuf_rd[1])
                            t_last_o = t_ttB
                        else:
                            ta = tmpsp[:, 0:VA_W]
                            tb = tmpsp[:, VA_W: 2 * VA_W]
                            t_add = P.emit("gpsimd", lambda ec, ta=ta, tb=tb: ec.tensor_add(
                                ta, ta, tb), deps=[t_ttA, t_ttB])
                            t_ttA = t_add
                            t_ttB = P.emit("vector", lambda ec, tb=tb, vin=vin, prep=prep: ec.tensor_mul(
                                tb, vin, prep),
                                deps=[sct_ready[(i, oi)], t_va1, t_add])
                            t_last_o = t_ttB
                        kv_rd[kb].append(t_last_o)
                        sct_rd = mx(*(sct_rd + [t_last_o]))
                    if t_ttB is not None:
                        scr_rd = mx(*(scr_rd + [t_ttB]))

                for hp in range(H // 2):
                    g = hp % 4
                    bank = psc_next()
                    scp = rotb[bank]
                    war = psc_readers[bank]
                    t_last = None
                    first = True
                    cck = hp
                    kdep = [qkv_evac[16 + cck]] if len(qkv_evac) > 16 + cck else qkv_done
                    for di, d in enumerate(deltas):
                        kb = qg - d
                        col = di * 256
                        lhs = kT[:, cck * T + kb * 128: cck * T + (kb + 1) * 128]
                        rhs = bass.AP(qZ, cck * CH + i * 128,
                                      [[2 * NDC * CH, 128], [NDC * CH, 2], [1, 128]])
                        out_ap = scp[:, col:col + 256]
                        t_last = P.emit("tensor", lambda ec, out_ap=out_ap, lhs=lhs, rhs=rhs, first=first: ec.matmul(
                            out_ap, lhs, rhs, start=first, stop=False),
                            deps=(war + kdep + qz_done.get(cck, qkv_done)) if first else [])
                        first = False
                        kv_rd[kb].append(t_last)
                    mrep = bass.AP(mb01, 0, [[256, 128], [128, nd], [0, 2], [1, 128]])
                    t_last = P.emit("tensor", lambda ec, scp=scp, width2=width2, mrep=mrep: ec.matmul(
                        scp[:, 0:width2], ident[:, :], mrep, start=False, stop=True),
                        deps=[t_m01, t_ident])
                    qT_rd.append(t_last)
                    if len(pending) >= 3:
                        emit_pv(pending.pop(0))
                    pm = pmask[:, g * 512: g * 512 + width2]
                    t_exp = P.emit("scalar", lambda ec, pm=pm, scp=scp, width2=width2: ec.activation(
                        pm, scp[:, 0:width2], AF.Exp, bias=-8.0, scale=0.125),
                        deps=[t_last] + pmask_rd[g] + t_cst)
                    psc_readers[bank] = [t_exp]
                    pending.append((t_exp, g, i, qg, deltas, nd, hp))
                while pending:
                    emit_pv(pending.pop(0))
                pv_last = pv_last_tile

                if sparse_here:
                    nsl = 2 if t_ttB is not None else 1
                    for bank2 in range(3):
                        lo = (0, 7 * SLOT, 14 * SLOT)[bank2]
                        hi = (7 * SLOT, 14 * SLOT, H * SLOT)[bank2]
                        for sl in range(nsl):
                            rhs = tmpsp[:, sl * VA_W + lo: sl * VA_W + hi]
                            out_ap = po[bank2][:, 0:hi - lo]
                            dep_t = t_ttA if sl == 0 else t_ttB
                            stp = (sl == nsl - 1)
                            t_pvf = P.emit("tensor", lambda ec, out_ap=out_ap, rhs=rhs, stp=stp: ec.matmul(
                                out_ap, ident[:, :], rhs, start=False, stop=stp),
                                deps=[dep_t, t_ident, pv_last[bank2]])
                            pv_last[bank2] = t_pvf
                    tt1_rd[0] = [pv_last[0], pv_last[1], pv_last[2]]
                    if nsl == 2:
                        scr_rd = mx(*(scr_rd + [pv_last[2]]))

                osb_tile = []
                for bank2 in range(3):
                    nh = (7, 7, 2)[bank2]
                    h0 = (0, 7, 14)[bank2]
                    dep_pv = [pv_last[bank2]]
                    dap = bass.AP(po[bank2], 0, [[512, 128], [SLOT, nh]])
                    t_rc = P.emit("vector", lambda ec, h0=h0, nh=nh, dap=dap: ec.reciprocal(
                        recip16[:, h0:h0 + nh], dap), deps=dep_pv)
                    rrep = bass.AP(recip16, h0, [[16, 128], [1, nh], [0, 64]])
                    osrc = bass.AP(po[bank2], 1, [[512, 128], [SLOT, nh], [1, 64]])
                    oout = xln[:, TPC * C + i * C + h0 * 64: TPC * C + i * C + (h0 + nh) * 64]
                    t_oe = P.emit("vector", lambda ec, oout=oout, osrc=osrc, rrep=rrep: ec.tensor_tensor(
                        oout, osrc, rrep, OP.mult), deps=[t_rc] + dep_pv + osb_rd)
                    rrd_set(5 + bank2, [t_oe])
                    osb_tile.append(t_oe)
                    osb_ready.append(t_oe)

                osb_dep = mx(*osb_tile)
                oT_ready_i, last_otr = emit_transpose(
                    xln, TPC * C, oT, 0,
                    [osb_dep[0] for _ in range(TPC)] if osb_dep else [None] * TPC,
                    oT_rd, tiles=[i])
                osb_rd = [last_otr]
                oT_ready_all += oT_ready_i
                oT_dep = mx(*oT_ready_i)
                for g in range(2):
                    bi = pb_next()
                    pb = pbank[bi]
                    t_mm = None
                    for dc in range(NDC):
                        lhs = oT[:, dc * CH + i * 128: dc * CH + (i + 1) * 128]
                        rhs = wproj[:, dc * C + g * 512: dc * C + (g + 1) * 512]
                        otd = [oT_ready_i[dc]] if len(oT_ready_i) == NDC else oT_dep
                        t_mm = P.emit("tensor", lambda ec, pb=pb, lhs=lhs, rhs=rhs, dc=dc: ec.matmul(
                            pb[:, 0:512], lhs, rhs, start=(dc == 0), stop=False),
                            deps=(pbr_get(bi) + otd + [t_wproj]) if dc == 0 else otd)
                    rhsb = bproj_row[0:32, g * 512:(g + 1) * 512]
                    t_mm = P.emit("tensor", lambda ec, pb=pb, rhsb=rhsb: ec.matmul(
                        pb[:, 0:512], ones32[0:32, :], rhsb, start=False, stop=True),
                        deps=[t_ones, t_bproj])
                    sc0 = 16 * i + 8
                    xap = bass.AP(xbuf, (r * TPC + i) * C + g * 512,
                                  [[2 * TPC * C, 128], [1, 512]])
                    acc = stats[:, sc0 + g: sc0 + g + 1]
                    t_r = P.emit("vector", lambda ec, pb=pb, xap=xap, acc=acc: ec.scalar_tensor_tensor(
                        xap, pb[:, 0:512], 1.0, xap, OP.mult, OP.add, accum_out=acc),
                        deps=[t_mm, t_norm1[i]] + stats_rd[2 * i + 1])
                    pbr_set(bi, [t_r])
                    t_resid1[2 * i + g] = t_r
                    stats_rd[2 * i + 1] = [t_r]
                if i == 0 and idx + 1 < len(sched):
                    ln1_stats_next = emit_ln1_stats(*sched[idx + 1])
                if i == 1 and idx + 1 < len(sched) and ln1_stats_next is not None:
                    ln1_cache = (ln1_stats_next, emit_ln1_tr(ln1_stats_next))
                    xT1_ready[0] = ln1_cache[1]
            oT_rd = mx(*t_resid1)

            pre2 = [[t_resid1[2 * i], t_resid1[2 * i + 1]] for i in range(TPC)]
            t_norm2 = emit_ln(c, 1, pre2, "stats")
            xln_w[1] = [t_norm2[-1]]

            if idx + 1 < len(sched):
                nb, nc2 = sched[idx + 1]
                qkv_st = emit_qkv_qk(nb, nc2)
            else:
                qkv_st = None

            xT2_ready, last_tr2 = emit_transpose(xln, TPC * C, xT, NDC * CH, t_norm2, xT_rd[1])
            xln_rd[1] = [last_tr2]
            osb_rd = [last_tr2]
            xT_rd[1] = []

            if qkv_st is not None:
                qkv_cache = emit_qkv_v(qkv_st)
                nb3, nc3 = sched[idx + 1]
                sp_steps, sp_finish, sparse_cache = build_sparse(
                    nc3, qkv_cache[1], qkv_cache[4])
            else:
                qkv_cache = None
                sp_steps, sp_finish, sparse_cache = [], None, None

            def pop_sp(n):
                for _ in range(n):
                    if sp_steps:
                        sp_steps.pop(0)()

            bi = pb_next()
            pbS = pbank[bi]
            t_mm = None
            for dc in range(NDC):
                lhs = win[:, dc * 64:(dc + 1) * 64]
                rhs = xT[:, NDC * CH + dc * CH:(NDC * CH) + (dc + 1) * CH]
                xtd = [xT2_ready[dc]] if len(xT2_ready) == NDC else xT2_ready
                t_mm = P.emit("tensor", lambda ec, pbS=pbS, lhs=lhs, rhs=rhs, dc=dc: ec.matmul(
                    pbS[0:64, 0:CH], lhs, rhs, start=(dc == 0), stop=(dc == NDC - 1)),
                    deps=(pbr_get(bi) + xtd + [t_win]) if dc == 0 else xtd)
            t_sT = P.emit("scalar", lambda ec, pbS=pbS: ec.activation(
                sTb[:, 0:CH], pbS[0:64, 0:CH], AF.Identity,
                bias=bin_col[:, 0:1], scale=1.0), deps=[t_mm, t_binc] + sTb_rd)
            pbr_set(bi, [t_sT])
            xT_rd[1] = [t_sT]

            biG = pb_next()
            pbG = pbank[biG]
            t_g_last = None
            HPOOL = (2, 3, 4, 7)
            hrot = [0]
            h_banks = {}
            t_hmm = {}

            def emit_hmm(ic, extra, bank=None):
                if bank is None:
                    bank = HPOOL[hrot[0] % 4]
                    hrot[0] += 1
                    if rotb[bank] is pbG:
                        bank = HPOOL[hrot[0] % 4]
                        hrot[0] += 1
                scp = rotb[bank]
                war = psc_readers[bank]
                lhs = w1exp[:, ic * 128:(ic + 1) * 128]
                t_h = P.emit("tensor", lambda ec, scp=scp, lhs=lhs: ec.matmul(
                    scp[:, 0:CH], lhs, sTb[:, 0:CH], start=True, stop=True),
                    deps=[t_sT] + war + extra + ([t_w12[-1]] if ic == 0 else []))
                h_banks[ic] = bank
                t_hmm[ic] = t_h
                return t_h

            NHB = 3
            for ic in range(NHB):
                sTb_rd = [emit_hmm(ic, [])]
                pop_sp(1)
            t_g_last = None
            for ic in range(8):
                bank = h_banks[ic]
                scp = rotb[bank]
                hs = ic % 2
                bap = b1col[:, ic:ic + 1]
                hout = hT[:, hs * CH:(hs + 1) * CH]
                if ic % 2 == 0:
                    t_hev = P.emit("scalar", lambda ec, hout=hout, scp=scp, bap=bap: ec.activation(
                        hout, scp[:, 0:CH], AF.Relu, bias=bap, scale=1.0),
                        deps=[t_hmm[ic], t_b1c] + hT_rd[hs])
                else:
                    t_hev = P.emit("vector", lambda ec, hout=hout, scp=scp, bap=bap: ec.tensor_scalar(
                        hout, scp[:, 0:CH], bap, 0.0, OP.add, OP.max),
                        deps=[t_hmm[ic], t_b1c] + hT_rd[hs])
                psc_readers[bank] = [t_hev]
                if ic + NHB < 8:
                    sTb_rd = [emit_hmm(ic + NHB, [t_hev], bank=bank)]
                lhs2 = w2exp[:, ic * 64:(ic + 1) * 64]
                hin = hT[:, hs * CH:(hs + 1) * CH]
                t_g_last = P.emit("tensor", lambda ec, pbG=pbG, lhs2=lhs2, hin=hin, ic=ic: ec.matmul(
                    pbG[0:64, 0:CH], lhs2, hin, start=(ic == 0), stop=(ic == 7)),
                    deps=[t_hev] + (pbr_get(biG) if ic == 0 else []))
                hT_rd[hs] = [t_g_last]
                pop_sp(2)
            t_gT = P.emit("scalar", lambda ec, pbG=pbG: ec.activation(
                gaugT[0:64, 0:CH], pbG[0:64, 0:CH], AF.Identity,
                bias=b2col[:, 0:1], scale=1.0), deps=[t_g_last, t_b2c, t_gaug1] + gaug_rd)
            pbr_set(biG, [t_gT])
            while sp_steps:
                sp_steps.pop(0)()
            if sp_finish is not None:
                sp_finish()

            t_resid2 = []
            for i in range(TPC):
                for g in range(2):
                    bi2 = pb_next()
                    pb = pbank[bi2]
                    lhs = gaugT[:, i * 128:(i + 1) * 128]
                    rhs = woutaug[:, g * 512:(g + 1) * 512]
                    t_mm = P.emit("tensor", lambda ec, pb=pb, lhs=lhs, rhs=rhs: ec.matmul(
                        pb[:, 0:512], lhs, rhs, start=True, stop=True),
                        deps=[t_gT, t_wo0, t_wo1] + pbr_get(bi2))
                    gaug_rd = [t_mm]
                    xap = bass.AP(xbuf, (r * TPC + i) * C + g * 512,
                                  [[2 * TPC * C, 128], [1, 512]])
                    t_r = P.emit("vector", lambda ec, pb=pb, xap=xap: ec.scalar_tensor_tensor(
                        xap, pb[:, 0:512], 1.0, xap, OP.mult, OP.add),
                        deps=[t_mm, t_resid1[2 * i + g], t_norm2[i]])
                    pbr_set(bi2, [t_r])
                    t_resid2.append(t_r)

            dstY = d_y[b, t0:t0 + CH, :].rearrange("(n p) c -> p n c", p=128)
            srcY = xbuf[:, r * TPC * C:(r + 1) * TPC * C]
            t_st = P.emit("sync", lambda ec, dstY=dstY, srcY=srcY: ec.dma_start(dstY, srcY),
                          deps=t_resid2, dma=True, dsem="dy%d" % r)
            x_ring_free[r] = [t_st]

        fin = [(k, P.cnt[k]) for k in Prog.DSEMS if P.cnt[k] > 0]
        P.wait_only("sync", fin)

        P.finalize()
        block = st.enter_context(nc.Block())

        @block.tensor
        def _(ec):
            P.replay("tensor", ec, sems)

        @block.scalar
        def _(ec):
            P.replay("scalar", ec, sems)

        @block.vector
        def _(ec):
            P.replay("vector", ec, sems)

        @block.gpsimd
        def _(ec):
            P.replay("gpsimd", ec, sems)

        @block.sync
        def _(ec):
            P.replay("sync", ec, sems)

    return nc


def host_aux(cfg, inputs):
    b_qkv = np.ascontiguousarray(inputs["L0_attn_b_qkv"], dtype=np.float32)
    b1 = np.ascontiguousarray(inputs["L0_ffn_b1"], dtype=np.float32)
    W1 = np.ascontiguousarray(inputs["L0_ffn_W1"], dtype=np.float32)
    W2 = np.ascontiguousarray(inputs["L0_ffn_W2"], dtype=np.float32)
    NCC = 2 * (cfg.C // 128)
    aux = {}
    aux["bqkv_col"] = np.ascontiguousarray(b_qkv[:2 * cfg.C].reshape(NCC, 128).T)
    aux["w1dup"] = np.ascontiguousarray(np.concatenate([W1, W1], axis=0))
    aux["w2dup"] = np.ascontiguousarray(np.concatenate([W2, W2], axis=0))
    b1c = np.empty((128, 8), dtype=np.float32)
    for ic in range(8):
        b1c[0:64, ic] = b1[:, 2 * ic]
        b1c[64:128, ic] = b1[:, 2 * ic + 1]
    aux["b1col"] = b1c
    return aux


def make_in_maps(cfg, inputs):
    a = aux_inputs(cfg)
    aux = host_aux(cfg, inputs)
    x = np.ascontiguousarray(inputs["L0_attn_x"], dtype=np.float32)
    base = {
        "wqkv": np.ascontiguousarray(inputs["L0_attn_W_qkv"], dtype=np.float32),
        "bqkv": np.ascontiguousarray(inputs["L0_attn_b_qkv"], dtype=np.float32),
        "bqkv_col": aux["bqkv_col"],
        "wproj": np.ascontiguousarray(inputs["L0_attn_W_proj"], dtype=np.float32),
        "bproj": np.ascontiguousarray(inputs["L0_attn_b_proj"], dtype=np.float32),
        "win": np.ascontiguousarray(inputs["L0_ffn_Win"], dtype=np.float32),
        "bin": np.ascontiguousarray(inputs["L0_ffn_bin_"], dtype=np.float32),
        "w1dup": aux["w1dup"],
        "b1col": aux["b1col"],
        "w2dup": aux["w2dup"],
        "b2": np.ascontiguousarray(inputs["L0_ffn_b2"], dtype=np.float32),
        "wout": np.ascontiguousarray(inputs["L0_ffn_Wout"], dtype=np.float32),
        "bout": np.ascontiguousarray(inputs["L0_ffn_bout"], dtype=np.float32),
        "m01": a["m01"],
        "mb01x": a["mb01x"],
        "ident": a["ident"],
    }
    in_maps = []
    for core in range(cfg.ncore):
        m = dict(base)
        m["x"] = np.ascontiguousarray(x[core * cfg.BPC:(core + 1) * cfg.BPC])
        in_maps.append(m)
    return in_maps


def kernel(**inputs):
    cfg = CFG()
    nc = build_program(cfg)
    in_maps = make_in_maps(cfg, inputs)
    res = run_bass_kernel_spmd(nc, in_maps, core_ids=list(range(cfg.ncore)))
    out = np.concatenate([r["y"] for r in res.results], axis=0)
    return np.ascontiguousarray(out.astype(np.float32))


# version 47 (final)
# speedup vs baseline: 1.2295x; 1.2295x over previous
import numpy as np

import concourse.bass as bass
import concourse.mybir as mybir
from concourse.bass_utils import run_bass_kernel_spmd

F32 = mybir.dt.float32
BF16 = mybir.dt.bfloat16
AF = mybir.ActivationFunctionType
OP = mybir.AluOpType

SLOT = 66


class CFG:
    def __init__(self, T=2048, CH=256, BPC=2, ncore=8):
        self.T = T
        self.CH = CH
        self.BPC = BPC
        self.ncore = ncore
        self.C = 1024
        self.H = 16
        self.DH = 64
        self.NT = T // 128
        self.NCH = T // CH
        self.TPC = CH // 128
        offs = [0]
        o = 1
        while o < T:
            offs.append(o)
            o *= 2
        self.offsets = offs
        self.big_offsets = [o for o in offs if o >= 256]
        self.NO = len(self.big_offsets)
        self.stage = 99


def fenwick_mask_np(T):
    m = np.zeros((T, T), dtype=bool)
    for i in range(T):
        j = i + 1
        while j <= T:
            m[i, j - 1] = True
            j += j & -j
        step = 1
        while i - step >= 0:
            m[i, i - step] = True
            step *= 2
    m &= np.tril(np.ones((T, T), dtype=bool))
    return m


def aux_inputs(cfg):
    big = fenwick_mask_np(384)
    m0 = big[128:256, 128:256]
    m1 = big[256:384, 128:256]
    m01 = np.concatenate([m0.T, m1.T], axis=1).astype(np.float32)
    mb0 = np.where(m0.T > 0.5, np.float32(0.0), np.float32(-1e30)).astype(np.float32)
    mb1 = np.where(m1.T > 0.5, np.float32(0.0), np.float32(-1e30)).astype(np.float32)
    mb01x = np.concatenate([mb0, mb1], axis=1)
    ident = np.eye(128, dtype=np.float32)
    return {"m01": m01, "mb01x": mb01x, "ident": ident}


class Prog:
    ENGS = ("tensor", "scalar", "vector", "gpsimd", "sync")
    DSEMS = ("dw", "dws", "dx0", "dx1", "dy0", "dy1")

    def __init__(self):
        self.q = {e: [] for e in self.ENGS}
        self.cnt = {k: 0 for k in ("t", "s", "v", "g", "y") + self.DSEMS}
        self.esem = {"tensor": "t", "scalar": "s", "vector": "v",
                     "gpsimd": "g", "sync": "y"}
        self.maxwait = {e: {} for e in self.ENGS}

    def emit(self, eng, fn, deps=(), dma=False, dsem=None):
        if dma:
            key = dsem
            self.cnt[key] += 16
            ticket = (key, self.cnt[key])
            inc = 16
        else:
            key = self.esem[eng]
            self.cnt[key] += 1
            ticket = (key, self.cnt[key])
            inc = 1
        waits = []
        mw = self.maxwait[eng]
        best = {}
        for d in deps:
            if d is None:
                continue
            sk, sv = d
            if sv > best.get(sk, 0):
                best[sk] = sv
        for sk, sv in best.items():
            if mw.get(sk, 0) >= sv:
                continue
            mw[sk] = sv
            waits.append((sk, sv))
        self.q[eng].append((fn, waits, key, inc))
        return ticket

    def wait_only(self, eng, deps):
        waits = []
        mw = self.maxwait[eng]
        for d in deps:
            if d is None:
                continue
            sk, sv = d
            if mw.get(sk, 0) >= sv:
                continue
            mw[sk] = sv
            waits.append((sk, sv))
        self.q[eng].append((None, waits, None, 0))

    def finalize(self):
        esems = set(self.esem.values())
        waited = {}
        for e in self.ENGS:
            for fn, waits, key, inc in self.q[e]:
                for sk, sv in waits:
                    if sk in esems:
                        waited.setdefault(sk, set()).add(sv)
        remap = {}
        for sk, vals in waited.items():
            sv = sorted(vals)
            remap[sk] = {v: i + 1 for i, v in enumerate(sv)}
        import bisect
        rank_upto = {}
        for sk, vals in waited.items():
            rank_upto[sk] = sorted(vals)
        for e in self.ENGS:
            nq = []
            cnt = {}
            for fn, waits, key, inc in self.q[e]:
                nw = []
                for sk, sv in waits:
                    if sk in esems:
                        lst = rank_upto.get(sk, [])
                        r = bisect.bisect_right(lst, sv)
                        if r > 0:
                            nw.append((sk, r))
                    else:
                        nw.append((sk, sv))
                if key is not None and key in esems:
                    c = cnt.get(key, 0) + 1
                    cnt[key] = c
                    keep = c in waited.get(key, ())
                    nq.append((fn, nw, key if keep else None, inc))
                else:
                    nq.append((fn, nw, key, inc))
            self.q[e] = nq

    def replay(self, eng, ec, sems):
        for fn, waits, key, inc in self.q[eng]:
            for (sk, sv) in waits:
                ec.wait_ge(sems[sk], sv)
            if fn is not None:
                instr = fn(ec)
                if key is not None:
                    instr.then_inc(sems[key], inc)


def mx(*tix):
    best = {}
    for t in tix:
        if t is None:
            continue
        k, v = t
        if v > best.get(k, 0):
            best[k] = v
    return [(k, v) for k, v in best.items()]


def build_program(cfg):
    nc = bass.Bass("TRN2", target_bir_lowering=False)
    T, CH, BPC, C, H = cfg.T, cfg.CH, cfg.BPC, cfg.C, cfg.H
    NT, NCH, TPC = cfg.NT, cfg.NCH, cfg.TPC
    NDC = C // 128
    NCC = 2 * NDC
    BIG = cfg.big_offsets
    NO = max(cfg.NO, 1)
    VA_W = H * SLOT

    d_x = nc.dram_tensor("x", [BPC, T, C], F32, kind="ExternalInput").ap()
    d_wqkv = nc.dram_tensor("wqkv", [C, 3 * C], F32, kind="ExternalInput").ap()
    d_bqkv = nc.dram_tensor("bqkv", [3 * C], F32, kind="ExternalInput").ap()
    d_bqkv_col = nc.dram_tensor("bqkv_col", [128, NCC], F32, kind="ExternalInput").ap()
    d_wproj = nc.dram_tensor("wproj", [C, C], F32, kind="ExternalInput").ap()
    d_bproj = nc.dram_tensor("bproj", [C], F32, kind="ExternalInput").ap()
    d_win = nc.dram_tensor("win", [C, 64], F32, kind="ExternalInput").ap()
    d_bin = nc.dram_tensor("bin", [64], F32, kind="ExternalInput").ap()
    d_w1dup = nc.dram_tensor("w1dup", [128, 16], F32, kind="ExternalInput").ap()
    d_b1col = nc.dram_tensor("b1col", [128, 8], F32, kind="ExternalInput").ap()
    d_w2dup = nc.dram_tensor("w2dup", [128, 16], F32, kind="ExternalInput").ap()
    d_b2 = nc.dram_tensor("b2", [64], F32, kind="ExternalInput").ap()
    d_wout = nc.dram_tensor("wout", [64, C], F32, kind="ExternalInput").ap()
    d_bout = nc.dram_tensor("bout", [C], F32, kind="ExternalInput").ap()
    d_m01 = nc.dram_tensor("m01", [128, 256], F32, kind="ExternalInput").ap()
    d_mb01 = nc.dram_tensor("mb01x", [128, 256], F32, kind="ExternalInput").ap()
    d_ident = nc.dram_tensor("ident", [128, 128], F32, kind="ExternalInput").ap()
    d_y = nc.dram_tensor("y", [BPC, T, C], F32, kind="ExternalOutput").ap()

    import contextlib
    st = contextlib.ExitStack()
    sbt = lambda n, shp, dt: st.enter_context(nc.sbuf_tensor(n, shp, dt))
    pst = lambda n, shp, dt: st.enter_context(nc.psum_tensor(n, shp, dt))

    with st:
        wqkv = sbt("s_wqkv", [128, NDC * 3 * C], BF16)
        bqkv_col = sbt("s_bqkvcol", [128, NCC], F32)
        bv_row = sbt("s_bvrow", [32, C], BF16)
        wproj = sbt("s_wproj", [128, NDC * C], BF16)
        bproj_row = sbt("s_bprojrow", [32, C], BF16)
        win = sbt("s_win", [128, NDC * 64], BF16)
        bin_col = sbt("s_bincol", [64, 1], F32)
        w1dup = sbt("s_w1dup", [128, 16], F32)
        b1col = sbt("s_b1col", [128, 8], F32)
        w2dup = sbt("s_w2dup", [128, 16], F32)
        b2col = sbt("s_b2col", [64, 1], F32)
        w1exp = sbt("s_w1exp", [64, 8 * 128], BF16)
        w2exp = sbt("s_w2exp", [128, 8 * 64], BF16)
        woutaug = sbt("s_woutaug", [65, C], BF16)
        mb01 = sbt("s_mb01", [128, 256], BF16)
        ident = sbt("s_ident", [128, 128], BF16)
        identf = sbt("s_identf", [128, 128], F32)
        ones32 = sbt("s_ones32", [32, 128], BF16)
        bo = sbt("s_bo", [128, NDC * 16], BF16)
        cst = sbt("s_cst", [128, 3], F32)

        kT = sbt("s_kT", [128, NDC * T], BF16)
        qZ = sbt("s_qZ", [128, 2 * NDC * CH], BF16)
        vaug = sbt("s_vaug", [128, NT * VA_W], BF16)
        xbuf = sbt("s_x", [128, 2 * TPC * C], F32)
        xln = sbt("s_xln", [128, 2 * TPC * C], F32)
        xT = sbt("s_xT", [128, 2 * NDC * CH], BF16)
        pmask = sbt("s_pmask", [128, 4 * 512], BF16)
        scb = sbt("s_scb", [16, NO * CH], F32)
        sct = sbt("s_sct", [128, TPC * NO * 16], BF16)
        tmpsp = sbt("s_tmpsp", [128, 2 * VA_W], BF16)
        ebuf = sbt("s_ebuf", [128, 2 * CH], BF16)
        oT = sbt("s_oT", [128, NDC * CH], BF16)
        sTb = sbt("s_sTb", [64, CH], BF16)
        gaugT = sbt("s_gaugT", [65, CH], BF16)
        hT = sbt("s_hT", [128, 2 * CH], BF16)
        stats = sbt("s_stats", [128, 16 * TPC], F32)
        recip16 = sbt("s_recip", [128, 16], F32)

        pbank = [pst("p_A", [128, 512], F32), pst("p_B", [128, 512], F32)]
        psc = [pst("p_sc0", [128, 512], F32), pst("p_sc1", [128, 512], F32),
               pst("p_sc2", [128, 512], F32)]
        po = [pst("p_o0", [128, 512], F32), pst("p_o1", [128, 512], F32),
              pst("p_o2", [128, 512], F32)]

        sems = {}
        for name in ("t", "s", "v", "g", "y") + Prog.DSEMS:
            sems[name] = st.enter_context(nc.semaphore("sem_" + name))

        nc.const_aps.register(F32, -8.0, cst[:, 0:1])
        nc.const_aps.register(F32, 1e-5, cst[:, 1:2])
        nc.const_aps.register(F32, 0.0, cst[:, 2:3])

        P = Prog()

        t_cexp = P.emit("vector", lambda ec: ec.memset(cst[:, 0:1], -8.0))
        t_ceps = P.emit("vector", lambda ec: ec.memset(cst[:, 1:2], 1e-5))
        t_czer = P.emit("vector", lambda ec: ec.memset(cst[:, 2:3], 0.0))
        t_cst = [t_cexp, t_ceps, t_czer]
        t_o32a = P.emit("vector", lambda ec: ec.memset(ones32[:, :], 0.0))
        t_ones = P.emit("vector", lambda ec: ec.memset(ones32[0:1, :], 1.0), deps=[t_o32a])
        t_qz0 = P.emit("vector", lambda ec: ec.memset(qZ[:, :], 0.0))
        t_bvz = P.emit("vector", lambda ec: ec.memset(bv_row[:, :], 0.0))
        t_bpz = P.emit("vector", lambda ec: ec.memset(bproj_row[:, :], 0.0))
        t_bo0 = P.emit("vector", lambda ec: ec.memset(bo[:, :], 0.0))
        t_bo = t_bo0
        for dc in range(NDC):
            for k in range(2):
                cidx = dc * 16 + 2 * dc + k
                ap = bo[64 * k:64 * k + 64, cidx:cidx + 1]
                t_bo = P.emit("vector", lambda ec, ap=ap: ec.memset(ap, 1.0),
                              deps=[t_bo0])
        t_va0 = P.emit("vector", lambda ec: ec.memset(vaug[:, :], 0.0))
        ones_ap = bass.AP(vaug, 0, [[VA_W * NT, 128], [VA_W, NT], [SLOT, H], [1, 1]])
        t_va1 = P.emit("vector", lambda ec: ec.memset(ones_ap, 1.0), deps=[t_va0])
        t_gaug1 = P.emit("vector", lambda ec: ec.memset(gaugT[64:65, :], 1.0))

        def cdma(dst_ap, src_ap):
            return P.emit("gpsimd", lambda ec: ec.dma_start(dst_ap, src_ap),
                          dma=True, dsem="dw")

        t_wqkv = cdma(wqkv[:, :], d_wqkv.rearrange("(n p) c -> p n c", p=128))
        t_wproj = cdma(wproj[:, :], d_wproj.rearrange("(n p) c -> p n c", p=128))
        t_win = cdma(win[:, :], d_win.rearrange("(n p) c -> p n c", p=128))
        t_bv = P.emit("gpsimd", lambda ec: ec.dma_start(
            bv_row[0:1, :], d_bqkv[2 * C:3 * C].unsqueeze(0)),
            deps=[t_bvz], dma=True, dsem="dw")
        t_bproj = P.emit("gpsimd", lambda ec: ec.dma_start(
            bproj_row[0:1, :], d_bproj.unsqueeze(0)),
            deps=[t_bpz], dma=True, dsem="dw")
        t_wo0 = cdma(woutaug[0:64, :], d_wout)
        t_wo1 = cdma(woutaug[64:65, :], d_bout.unsqueeze(0))
        t_m01 = cdma(mb01[:, :], d_mb01)
        t_ident = cdma(ident[:, :], d_ident)

        t_identf = P.emit("sync", lambda ec: ec.dma_start(identf[:, :], d_ident), dma=True, dsem="dws")
        t_bqc = P.emit("sync", lambda ec: ec.dma_start(bqkv_col[:, :], d_bqkv_col), dma=True, dsem="dws")
        t_binc = P.emit("sync", lambda ec: ec.dma_start(bin_col[:, :], d_bin.unsqueeze(1)), dma=True, dsem="dws")
        t_w1d = P.emit("sync", lambda ec: ec.dma_start(w1dup[:, :], d_w1dup), dma=True, dsem="dws")
        t_b1c = P.emit("sync", lambda ec: ec.dma_start(b1col[:, :], d_b1col), dma=True, dsem="dws")
        t_w2d = P.emit("sync", lambda ec: ec.dma_start(w2dup[:, :], d_w2dup), dma=True, dsem="dws")
        t_b2c = P.emit("sync", lambda ec: ec.dma_start(b2col[:, :], d_b2.unsqueeze(1)), dma=True, dsem="dws")

        t_dw_all = ("dw", P.cnt["dw"])
        t_dws_all = ("dws", P.cnt["dws"])
        t_wqkv = t_wproj = t_win = t_bv = t_bproj = t_wo0 = t_wo1 = t_m01 = t_ident = t_dw_all
        t_identf = t_bqc = t_binc = t_w1d = t_b1c = t_w2d = t_b2c = t_dws_all

        t_w12 = []
        for ic in range(8):
            for k in range(2):
                i = 2 * ic + k
                dst1 = w1exp[0:64, ic * 128 + 64 * k: ic * 128 + 64 * k + 64]
                s1 = w1dup[0:64, i:i + 1]
                t_w12.append(P.emit(
                    "vector", lambda ec, dst1=dst1, s1=s1: ec.tensor_scalar_mul(
                        dst1, ident[0:64, 0:64], s1),
                    deps=[t_ident, t_w1d]))
                dst2 = w2exp[64 * k:64 * k + 64, ic * 64: ic * 64 + 64]
                src2 = ident[64 * k:64 * k + 64, 64 * k:64 * k + 64]
                s2 = w2dup[64 * k:64 * k + 64, i:i + 1]
                t_w12.append(P.emit(
                    "vector", lambda ec, dst2=dst2, src2=src2, s2=s2: ec.tensor_scalar_mul(
                        dst2, src2, s2),
                    deps=[t_ident, t_w2d]))

        rot = [None, None, None, None, None]
        psc_readers = [[] for _ in range(8)]
        psc_i = [0]

        SCORE_POOL = (0, 1, 2, 3)

        def psc_next():
            bk = SCORE_POOL[psc_i[0] % len(SCORE_POOL)]
            psc_i[0] += 1
            return bk

        ROT6 = (2, 3, 4, 5, 6, 7)
        r6_i = [0]

        def rot6_next():
            bk = ROT6[r6_i[0] % 6]
            r6_i[0] += 1
            return bk

        pb_i = [0]
        QKV_POOL = (3, 4, 5, 6, 7)
        qkv_i = [0]

        def pb_next():
            bk = pb_i[0] % 2
            pb_i[0] += 1
            return bk

        def qkv_next():
            bk = QKV_POOL[qkv_i[0] % len(QKV_POOL)]
            qkv_i[0] += 1
            return bk

        rotb = psc + pbank + po

        def pbr_get(bi):
            return psc_readers[3 + bi]

        def pbr_set(bi, lst):
            psc_readers[3 + bi] = lst

        def rrd_get(bk):
            return psc_readers[bk]

        def rrd_set(bk, lst):
            psc_readers[bk] = lst

        kv_rd = [[] for _ in range(NT)]
        qT_rd = []
        xT_rd = [[], []]
        xln_rd = [[], []]
        xln_w = [[], []]
        ebuf_rd = [[], [], []]
        scr_rd = []
        tt1_rd = [[]]
        hT_rd = [[], []]
        pmask_rd = [[], [], [], []]
        praw_rd = [[], []]
        x_ring_free = [[], []]
        oT_rd = []
        sTb_rd = []
        gaug_rd = []
        scb_rd = []
        sct_rd = []
        osb_rd = []
        stats_rd = [[] for _ in range(2 * TPC)]
        tev_alt = [0]

        def po_slot2(h):
            if h < 7:
                return 0, h * SLOT
            if h < 14:
                return 1, (h - 7) * SLOT
            return 2, (h - 14) * SLOT

        def xsl(c, i):
            rr = c % 2
            return xbuf[:, (rr * TPC + i) * C:(rr * TPC + i + 1) * C]

        def emit_ln(c, slot, pre_deps_tile, sumx_mode):
            ts_sx, ts_sq, ts_m2, ts_var, ts_rs = [], [], [], [], []
            for i in range(TPC):
                sc0 = 16 * i + 8 * slot
                sb_i = 2 * i + slot
                xap = xsl(c, i)
                xlap = xln[:, (slot * TPC + i) * C:(slot * TPC + i + 1) * C]
                pre = pre_deps_tile[i]
                if sumx_mode == "reduce":
                    t_sx = P.emit("vector", lambda ec, xap=xap, sc0=sc0: ec.reduce_sum(
                        stats[:, sc0:sc0 + 1], xap, mybir.AxisListType.X),
                        deps=pre + stats_rd[sb_i])
                else:
                    t_sx = P.emit("vector", lambda ec, sc0=sc0: ec.tensor_add(
                        stats[:, sc0:sc0 + 1], stats[:, sc0:sc0 + 1],
                        stats[:, sc0 + 1:sc0 + 2]), deps=pre + stats_rd[sb_i])
                t_sq = P.emit("scalar", lambda ec, xap=xap, xlap=xlap, sc0=sc0: ec.activation(
                    xlap, xap, AF.Square, accum_out=stats[:, sc0 + 1:sc0 + 2]),
                    deps=pre + xln_rd[slot] + xln_w[slot] + [t_sx] + t_cst)
                t_m2 = P.emit("vector", lambda ec, sc0=sc0: ec.scalar_tensor_tensor(
                    stats[:, sc0 + 4:sc0 + 5], stats[:, sc0:sc0 + 1], 1.0 / (C * C),
                    stats[:, sc0:sc0 + 1], OP.mult, OP.mult), deps=[t_sx])
                ts_sx.append(t_sx)
                ts_sq.append(t_sq)
                ts_m2.append(t_m2)
            for i in range(TPC):
                sc0 = 16 * i + 8 * slot
                ts_var.append(P.emit("vector", lambda ec, sc0=sc0: ec.scalar_tensor_tensor(
                    stats[:, sc0 + 5:sc0 + 6], stats[:, sc0 + 1:sc0 + 2], 1.0 / C,
                    stats[:, sc0 + 4:sc0 + 5], OP.mult, OP.subtract),
                    deps=[ts_sq[i], ts_m2[i]]))
            for i in range(TPC):
                sc0 = 16 * i + 8 * slot
                t_ln = P.emit("scalar", lambda ec, sc0=sc0: ec.activation(
                    stats[:, sc0 + 6:sc0 + 7], stats[:, sc0 + 5:sc0 + 6],
                    AF.Ln, bias=1e-5), deps=[ts_var[i]] + t_cst)
                ts_rs.append(P.emit("scalar", lambda ec, sc0=sc0: ec.activation(
                    stats[:, sc0 + 7:sc0 + 8], stats[:, sc0 + 6:sc0 + 7],
                    AF.Exp, scale=-0.5), deps=[t_ln]))
            t_norm = []
            for i in range(TPC):
                sc0 = 16 * i + 8 * slot
                sb_i = 2 * i + slot
                xap = xsl(c, i)
                xlap = xln[:, (slot * TPC + i) * C:(slot * TPC + i + 1) * C]
                t_mr = P.emit("vector", lambda ec, sc0=sc0: ec.scalar_tensor_tensor(
                    stats[:, sc0 + 4:sc0 + 5], stats[:, sc0:sc0 + 1], 1.0 / C,
                    stats[:, sc0 + 7:sc0 + 8], OP.mult, OP.mult),
                    deps=[ts_rs[i], ts_var[i]])
                t_nm = P.emit("vector", lambda ec, xap=xap, xlap=xlap, sc0=sc0: ec.tensor_scalar(
                    xlap, xap, stats[:, sc0 + 7:sc0 + 8], stats[:, sc0 + 4:sc0 + 5],
                    OP.mult, OP.subtract), deps=[t_mr, ts_sq[i]])
                t_norm.append(t_nm)
                stats_rd[sb_i] = [t_nm]
            return t_norm

        def emit_transpose(src_t, src_base, dst_t, dst_base, t_dep_tiles, slot_rd,
                           tiles=None, dep_by_dc=None):
            if tiles is None:
                tiles = list(range(TPC))
            evs = []
            last_tr = None
            for dc in range(NDC):
                bank = rot6_next()
                dst = rotb[bank]
                war = psc_readers[bank]
                trs = []
                for si, i in enumerate(tiles):
                    src_ap = src_t[:, src_base + i * C + dc * 128: src_base + i * C + (dc + 1) * 128]
                    if dep_by_dc is not None:
                        dps = dep_by_dc(dc) + (([t_identf] + war) if si == 0 else [])
                    else:
                        dps = ([t_dep_tiles[i], t_identf] + war) if si == 0 else [t_dep_tiles[i]]
                    t_tr = P.emit("tensor", lambda ec, dst=dst, si=si, src_ap=src_ap: ec.transpose(
                        dst[:, si * 128:(si + 1) * 128], src_ap, identf[:, :]),
                        deps=dps)
                    trs.append(t_tr)
                    last_tr = t_tr
                w = len(tiles) * 128
                out_ap = dst_t[:, dst_base + dc * CH + tiles[0] * 128:
                               dst_base + dc * CH + tiles[0] * 128 + w]
                t_ev = P.emit("vector", lambda ec, dst=dst, w=w, out_ap=out_ap: ec.tensor_copy(
                    out_ap, dst[:, 0:w]), deps=trs + slot_rd)
                psc_readers[bank] = [t_ev]
                evs.append(t_ev)
            return evs, last_tr

        def emit_ln1_stats(b, c):
            t_xl = xloads[(b, c)]
            ts_sq, ts_m2, ts_var, ts_rs = [], [], [], []
            for i in range(TPC):
                sc0 = 16 * i
                sb_i = 2 * i
                xap = xsl(c, i)
                xlap = xln[:, i * C:(i + 1) * C]
                t_sx = P.emit("scalar", lambda ec, xap=xap, xlap=xlap, sc0=sc0: ec.activation(
                    xlap, xap, AF.Identity, accum_out=stats[:, sc0:sc0 + 1]),
                    deps=[t_xl] + stats_rd[sb_i] + xln_rd[0] + xln_w[0] + t_cst)
                t_sq = P.emit("scalar", lambda ec, xap=xap, xlap=xlap, sc0=sc0: ec.activation(
                    xlap, xap, AF.Square, accum_out=stats[:, sc0 + 1:sc0 + 2]),
                    deps=[t_sx])
                t_m2 = P.emit("vector", lambda ec, sc0=sc0: ec.scalar_tensor_tensor(
                    stats[:, sc0 + 4:sc0 + 5], stats[:, sc0:sc0 + 1], 1.0 / (C * C),
                    stats[:, sc0:sc0 + 1], OP.mult, OP.mult), deps=[t_sx])
                ts_sq.append(t_sq)
                ts_m2.append(t_m2)
            for i in range(TPC):
                sc0 = 16 * i
                ts_var.append(P.emit("vector", lambda ec, sc0=sc0: ec.scalar_tensor_tensor(
                    stats[:, sc0 + 5:sc0 + 6], stats[:, sc0 + 1:sc0 + 2], 1.0 / C,
                    stats[:, sc0 + 4:sc0 + 5], OP.mult, OP.subtract),
                    deps=[ts_sq[i], ts_m2[i]]))
            for i in range(TPC):
                sc0 = 16 * i
                t_ln = P.emit("scalar", lambda ec, sc0=sc0: ec.activation(
                    stats[:, sc0 + 6:sc0 + 7], stats[:, sc0 + 5:sc0 + 6],
                    AF.Ln, bias=1e-5), deps=[ts_var[i]] + t_cst)
                ts_rs.append(P.emit("scalar", lambda ec, sc0=sc0: ec.activation(
                    stats[:, sc0 + 7:sc0 + 8], stats[:, sc0 + 6:sc0 + 7],
                    AF.Exp, scale=-0.5), deps=[t_ln]))
            t_norm = []
            for i in range(TPC):
                sc0 = 16 * i
                sb_i = 2 * i
                xap = xsl(c, i)
                xlap = xln[:, i * C:(i + 1) * C]
                t_mr = P.emit("vector", lambda ec, sc0=sc0: ec.scalar_tensor_tensor(
                    stats[:, sc0 + 4:sc0 + 5], stats[:, sc0:sc0 + 1], 1.0 / C,
                    stats[:, sc0 + 7:sc0 + 8], OP.mult, OP.mult),
                    deps=[ts_rs[i], ts_var[i]])
                t_nm = P.emit("vector", lambda ec, xap=xap, xlap=xlap, sc0=sc0: ec.tensor_scalar(
                    xlap, xap, stats[:, sc0 + 7:sc0 + 8], stats[:, sc0 + 4:sc0 + 5],
                    OP.mult, OP.subtract), deps=[t_mr, ts_sq[i]])
                t_norm.append(t_nm)
                stats_rd[sb_i] = [t_nm]
            xln_w[0] = [t_norm[-1]]
            return t_norm

        def emit_ln1_tr(t_norm1):
            evs, last_tr = emit_transpose(xln, 0, xT, 0, t_norm1, xT_rd[0])
            xln_rd[0] = [last_tr]
            xT_rd[0] = []
            return evs

        def emit_ln1_block(b, c):
            t_norm1 = emit_ln1_stats(b, c)
            return t_norm1, emit_ln1_tr(t_norm1)

        SPB = (5, 6)

        def build_sparse(c2, qz_done2, qkv_done2):
            t0s = c2 * CH
            tile0s = t0s // 128
            sct_ready2 = {}
            steps = []
            valid_big = [(oi, o) for oi, o in enumerate(BIG) if t0s + CH > o]
            if not valid_big:
                return steps, None, sct_ready2
            per_bank = max(512 // CH, 1)
            nbank = (len(valid_big) + per_bank - 1) // per_bank
            sc_banks = [SPB[k] for k in range(nbank)]
            st_cell = {"war": None, "final": None}

            def get_war():
                if st_cell["war"] is None:
                    w = []
                    for bk in sc_banks:
                        w += psc_readers[bk]
                    st_cell["war"] = w
                return st_cell["war"]

            ei_cell = [0]
            for vi, (oi, o) in enumerate(valid_big):
                scp = rotb[sc_banks[vi // per_bank]]
                col0 = (vi % per_bank) * CH
                a = max(0, o - t0s)
                w = CH - a
                for dc in range(NDC):
                    def step(vi=vi, oi=oi, o=o, dc=dc, scp=scp, col0=col0, a=a, w=w):
                        es = ei_cell[0] % 3
                        ei_cell[0] += 1
                        kap_lo = kT[0:64, dc * T + t0s + a - o: dc * T + t0s + a - o + w]
                        kap_hi = kT[64:128, dc * T + t0s + a - o: dc * T + t0s + a - o + w]
                        qlo = qZ[0:64, dc * CH + a: dc * CH + a + w]
                        qhi = qZ[64:128, NDC * CH + dc * CH + a: NDC * CH + dc * CH + a + w]
                        if es != 1:
                            e0 = (es // 2) * CH
                            elo = ebuf[0:64, e0 + a:e0 + a + w]
                            ehi = ebuf[64:128, e0 + a:e0 + a + w]
                            e_in = ebuf[:, e0 + a:e0 + a + w]
                        else:
                            elo = tmpsp[0:64, VA_W + a: VA_W + a + w]
                            ehi = tmpsp[64:128, VA_W + a: VA_W + a + w]
                            e_in = tmpsp[:, VA_W + a: VA_W + a + w]
                        dq = qz_done2.get(dc, qkv_done2)
                        lo_eng = "gpsimd" if dc % 2 == 0 else "vector"
                        t_e1 = P.emit(lo_eng, lambda ec, elo=elo, qlo=qlo, kap_lo=kap_lo: ec.tensor_mul(
                            elo, qlo, kap_lo), deps=dq + ebuf_rd[es] + (scr_rd if es == 1 else []))
                        t_e2 = P.emit("vector", lambda ec, ehi=ehi, qhi=qhi, kap_hi=kap_hi: ec.tensor_mul(
                            ehi, qhi, kap_hi), deps=dq + ebuf_rd[es] + (scr_rd if es == 1 else []))
                        boap = bo[:, dc * 16:(dc + 1) * 16]
                        out_ap = scp[0:16, col0 + a: col0 + a + w]
                        t_mm = P.emit("tensor", lambda ec, out_ap=out_ap, boap=boap, e_in=e_in, dc=dc: ec.matmul(
                            out_ap, boap, e_in, start=(dc == 0), stop=(dc == NDC - 1)),
                            deps=([t_e1, t_e2, t_bo] + get_war()) if dc == 0 else [t_e1, t_e2])
                        ebuf_rd[es] = [t_mm]
                        st_cell["final"] = t_mm
                        for jt in range((t0s + a - o) // 128, (t0s + a - o + w + 127) // 128):
                            kv_rd[jt].append(t_e1)
                            kv_rd[jt].append(t_e2)
                        qT_rd.append(t_e1)
                        qT_rd.append(t_e2)
                    steps.append(step)

            def finish():
                nonlocal scb_rd, sct_rd
                t_exp_by_oi = {}
                for vi, (oi, o) in enumerate(valid_big):
                    scp = rotb[sc_banks[vi // per_bank]]
                    col0 = (vi % per_bank) * CH
                    a = max(0, o - t0s)
                    w = CH - a
                    in_ap = scp[0:16, col0 + a: col0 + a + w]
                    out_ap = scb[0:16, oi * CH + a: oi * CH + a + w]
                    t_exp = P.emit("scalar", lambda ec, in_ap=in_ap, out_ap=out_ap: ec.activation(
                        out_ap, in_ap, AF.Exp, bias=-8.0, scale=0.125),
                        deps=[st_cell["final"]] + t_cst + scb_rd)
                    t_exp_by_oi[oi] = t_exp
                for bk in sc_banks:
                    psc_readers[bk] = list(t_exp_by_oi.values())
                for i in range(TPC):
                    qg = tile0s + i
                    for oi, o in valid_big:
                        if qg * 128 < o:
                            continue
                        bank = rot6_next()
                        dst = rotb[bank]
                        src_ap = scb[0:16, oi * CH + i * 128: oi * CH + (i + 1) * 128]
                        out_ap = sct[:, (i * NO + oi) * 16:(i * NO + oi) * 16 + 16]
                        idap = identf[0:16, 0:16]
                        t_tr = P.emit("tensor", lambda ec, dst=dst, src_ap=src_ap, idap=idap: ec.transpose(
                            dst[:, 0:16], src_ap, idap),
                            deps=[t_exp_by_oi[oi], t_identf] + psc_readers[bank])
                        scb_rd = [t_tr]
                        t_ev = P.emit("vector", lambda ec, dst=dst, out_ap=out_ap: ec.tensor_copy(
                            out_ap, dst[:, 0:16]), deps=[t_tr] + sct_rd)
                        psc_readers[bank] = [t_ev]
                        sct_ready2[(i, oi)] = t_ev

            return steps, finish, sct_ready2

        sched = [(b, c) for b in range(BPC) for c in range(NCH)]
        xloads = {}

        def emit_xload(b, c):
            rr = c % 2
            srcx = d_x[b, c * CH:(c + 1) * CH, :].rearrange("(n p) c -> p n c", p=128)
            dst_x = xbuf[:, rr * TPC * C:(rr + 1) * TPC * C]
            t = P.emit("sync", lambda ec, dst_x=dst_x, srcx=srcx: ec.dma_start(dst_x, srcx),
                       deps=x_ring_free[rr], dma=True, dsem="dx%d" % rr)
            x_ring_free[rr] = []
            xloads[(b, c)] = t

        ln1_cache = None
        qkv_cache = None
        sparse_cache = None
        xT1_ready = [[]]
        for idx, (b, c) in enumerate(sched):
            r = c % 2
            t0 = c * CH
            tile0 = t0 // 128
            if idx == 0:
                emit_xload(b, c)
            if idx + 1 < len(sched):
                emit_xload(*sched[idx + 1])

            if ln1_cache is not None:
                t_norm1, xT_ready = ln1_cache
                xT1_ready[0] = xT_ready

            def emit_qkv_qk(b_q, c_q):
                nonlocal qT_rd
                t0 = c_q * CH
                tile0 = t0 // 128
                war_kv = []
                for i in range(TPC):
                    war_kv += kv_rd[tile0 + i]
                qkv_evac = []
                qz_done = {}
                v_w = []
                vw_tile = {}
                for cc in range(NCC):
                    bi = qkv_next()
                    pb = rotb[bi]
                    t_mm = None
                    for dc in range(NDC):
                        lhs = wqkv[:, dc * 3 * C + cc * 128: dc * 3 * C + (cc + 1) * 128]
                        rhs = xT[:, dc * CH: (dc + 1) * CH]
                        xtd = [xT1_ready[0][dc]] if len(xT1_ready[0]) == NDC else xT1_ready[0]
                        t_mm = P.emit("tensor", lambda ec, pb=pb, lhs=lhs, rhs=rhs, dc=dc: ec.matmul(
                            pb[:, 0:CH], lhs, rhs, start=(dc == 0), stop=(dc == NDC - 1)),
                            deps=(rrd_get(bi) + xtd + [t_wqkv]) if dc == 0 else xtd)
                    if cc < NDC:
                        dlo = qZ[0:64, cc * CH:(cc + 1) * CH]
                        dhi = qZ[64:128, NDC * CH + cc * CH: NDC * CH + (cc + 1) * CH]
                        blo = bqkv_col[0:64, cc:cc + 1]
                        bhi = bqkv_col[64:128, cc:cc + 1]
                        t_ev1 = P.emit("scalar", lambda ec, dlo=dlo, blo=blo, pb=pb: ec.activation(
                            dlo, pb[0:64, 0:CH], AF.Identity, bias=blo, scale=1.0),
                            deps=[t_mm, t_bqc, t_qz0] + qT_rd)
                        t_ev = P.emit("vector", lambda ec, dhi=dhi, bhi=bhi, pb=pb: ec.tensor_scalar_add(
                            dhi, pb[64:128, 0:CH], bhi),
                            deps=[t_mm, t_bqc, t_qz0] + qT_rd)
                        rrd_set(bi, [t_ev1, t_ev])
                        qkv_evac.append(t_ev1)
                        qkv_evac.append(t_ev)
                        qz_done[cc] = [t_ev1, t_ev]
                        continue
                    ck = cc - NDC
                    dst = kT[:, ck * T + t0: ck * T + t0 + CH]
                    bcol = bqkv_col[:, cc:cc + 1]
                    if ck % 2 == 0:
                        t_ev = P.emit("scalar", lambda ec, dst=dst, bcol=bcol, pb=pb: ec.activation(
                            dst, pb[:, 0:CH], AF.Identity, bias=bcol, scale=1.0),
                            deps=[t_mm, t_bqc] + war_kv)
                    else:
                        t_ev = P.emit("vector", lambda ec, dst=dst, bcol=bcol, pb=pb: ec.tensor_scalar_add(
                            dst, pb[:, 0:CH], bcol),
                            deps=[t_mm, t_bqc] + war_kv)
                    rrd_set(bi, [t_ev])
                    qkv_evac.append(t_ev)
                qT_rd = []
                return dict(t0=t0, tile0=tile0, war_kv=war_kv, qkv_evac=qkv_evac, qz_done=qz_done, v_w=v_w, vw_tile=vw_tile)

            def emit_qkv_v(st_q):
                nonlocal qT_rd
                t0 = st_q["t0"]
                tile0 = st_q["tile0"]
                war_kv = st_q["war_kv"]
                qkv_evac = st_q["qkv_evac"]
                qz_done = st_q["qz_done"]
                v_w = st_q["v_w"]
                vw_tile = st_q["vw_tile"]
                for i in range(TPC):
                    jt = tile0 + i
                    for g in range(2):
                        bi = qkv_next()
                        pb = rotb[bi]
                        t_mm = None
                        for dc in range(NDC):
                            lhs = xT[:, dc * CH + i * 128: dc * CH + (i + 1) * 128]
                            rhs = wqkv[:, dc * 3 * C + 2 * C + g * 512: dc * 3 * C + 2 * C + (g + 1) * 512]
                            xtd = [xT1_ready[0][dc]] if len(xT1_ready[0]) == NDC else xT1_ready[0]
                            t_mm = P.emit("tensor", lambda ec, pb=pb, lhs=lhs, rhs=rhs, dc=dc: ec.matmul(
                                pb[:, 0:512], lhs, rhs, start=(dc == 0), stop=False),
                                deps=(rrd_get(bi) + xtd + [t_wqkv]) if dc == 0 else xtd)
                        rhsb = bv_row[0:32, g * 512:(g + 1) * 512]
                        t_mm = P.emit("tensor", lambda ec, pb=pb, rhsb=rhsb: ec.matmul(
                            pb[:, 0:512], ones32[0:32, :], rhsb, start=False, stop=True),
                            deps=[t_ones, t_bv])
                        h0 = g * 8
                        dst = bass.AP(vaug, jt * VA_W + h0 * SLOT + 1,
                                      [[NT * VA_W, 128], [SLOT, 8], [1, 64]])
                        srcp = bass.AP(pb, 0, [[512, 128], [64, 8], [1, 64]])
                        t_ev = P.emit("vector", lambda ec, dst=dst, srcp=srcp: ec.tensor_copy(dst, srcp),
                                      deps=[t_mm, t_va1] + war_kv)
                        rrd_set(bi, [t_ev])
                        v_w.append(t_ev)
                        vw_tile.setdefault(jt, []).append(t_ev)
                xT_rd[0] = [v_w[-1]]
                for i in range(TPC):
                    kv_rd[tile0 + i] = []

                

                return qkv_evac, qz_done, v_w, vw_tile, mx(*(qkv_evac + v_w))

            if qkv_cache is None:
                t_norm1, xT_ready = ln1_cache if ln1_cache else emit_ln1_block(b, c)
                xT1_ready[0] = xT_ready
                qkv_cache = emit_qkv_v(emit_qkv_qk(b, c))
            qkv_evac, qz_done, v_w, vw_tile, qkv_done = qkv_cache

            if sparse_cache is None:
                sp_steps0, sp_fin0, sct_ready = build_sparse(c, qz_done, qkv_done)
                for s_ in sp_steps0:
                    s_()
                if sp_fin0 is not None:
                    sp_fin0()
            else:
                sct_ready = sparse_cache

            if idx + 1 >= len(sched):
                ln1_stats_next = None
                ln1_cache = None

            osb_ready = []
            t_resid1 = [None] * (2 * TPC)
            oT_ready_all = []

            def emit_pv(job):
                t_exp_j, g_j, i_j, qg_j, deltas_j, nd_j, hp_j = job
                t_pv = None
                for hh in range(2):
                    h = 2 * hp_j + hh
                    bank2, scol = po_slot2(h)
                    for di, d in enumerate(deltas_j):
                        kb = qg_j - d
                        pcol = g_j * 512 + di * 256 + hh * 128
                        lhs = pmask[:, pcol:pcol + 128]
                        rhs = vaug[:, kb * VA_W + h * SLOT: kb * VA_W + h * SLOT + SLOT]
                        out_ap = po[bank2][:, scol:scol + SLOT]
                        st = not bank_started[bank2]
                        bank_started[bank2] = True
                        stp = ((not sparse_here) and (di == nd_j - 1)
                               and h in (6, 13, 15))
                        t_pv = P.emit("tensor", lambda ec, out_ap=out_ap, lhs=lhs, rhs=rhs, st=st, stp=stp: ec.matmul(
                            out_ap, lhs, rhs, start=st, stop=stp),
                            deps=([t_exp_j] + psc_readers[5 + bank2] + vw_tile.get(kb, [])) if st else [t_exp_j])
                        kv_rd[kb].append(t_pv)
                    pv_last_tile[bank2] = t_pv
                pmask_rd[g_j] = [t_pv]

            pending = []
            for i in range(TPC):
                qg = tile0 + i
                deltas = [d for d in (0, 1) if qg - d >= 0]
                nd = len(deltas)
                width2 = 2 * nd * 128
                pv_last_tile = [None, None, None]
                bank_started = [False, False, False]
                valid_o = [(oi, o) for oi, o in enumerate(BIG) if qg * 128 >= o]
                sparse_here = len(valid_o) > 0
                ss = 0

                t_ttA = None
                t_ttB = None
                if sparse_here:
                    for j, (oi, o) in enumerate(valid_o):
                        kb = qg - o // 128
                        prep = bass.AP(sct, (i * NO + oi) * 16,
                                       [[TPC * NO * 16, 128], [1, H], [0, SLOT]])
                        vin = vaug[:, kb * VA_W: (kb + 1) * VA_W]
                        if j == 0:
                            tout = tmpsp[:, 0:VA_W]
                            t_ttA = P.emit("gpsimd", lambda ec, tout=tout, vin=vin, prep=prep: ec.tensor_mul(
                                tout, vin, prep),
                                deps=[sct_ready[(i, oi)], t_va1] + tt1_rd[0])
                            t_last_o = t_ttA
                        elif j == 1:
                            tout = tmpsp[:, VA_W: 2 * VA_W]
                            t_ttB = P.emit("vector", lambda ec, tout=tout, vin=vin, prep=prep: ec.tensor_mul(
                                tout, vin, prep),
                                deps=[sct_ready[(i, oi)], t_va1] + scr_rd + ebuf_rd[1])
                            t_last_o = t_ttB
                        else:
                            ta = tmpsp[:, 0:VA_W]
                            tb = tmpsp[:, VA_W: 2 * VA_W]
                            t_add = P.emit("gpsimd", lambda ec, ta=ta, tb=tb: ec.tensor_add(
                                ta, ta, tb), deps=[t_ttA, t_ttB])
                            t_ttA = t_add
                            t_ttB = P.emit("vector", lambda ec, tb=tb, vin=vin, prep=prep: ec.tensor_mul(
                                tb, vin, prep),
                                deps=[sct_ready[(i, oi)], t_va1, t_add])
                            t_last_o = t_ttB
                        kv_rd[kb].append(t_last_o)
                        sct_rd = mx(*(sct_rd + [t_last_o]))
                    if t_ttB is not None:
                        scr_rd = mx(*(scr_rd + [t_ttB]))

                for hp in range(H // 2):
                    g = hp % 4
                    bank = psc_next()
                    scp = rotb[bank]
                    war = psc_readers[bank]
                    t_last = None
                    first = True
                    cck = hp
                    kdep = [qkv_evac[16 + cck]] if len(qkv_evac) > 16 + cck else qkv_done
                    for di, d in enumerate(deltas):
                        kb = qg - d
                        col = di * 256
                        lhs = kT[:, cck * T + kb * 128: cck * T + (kb + 1) * 128]
                        rhs = bass.AP(qZ, cck * CH + i * 128,
                                      [[2 * NDC * CH, 128], [NDC * CH, 2], [1, 128]])
                        out_ap = scp[:, col:col + 256]
                        t_last = P.emit("tensor", lambda ec, out_ap=out_ap, lhs=lhs, rhs=rhs, first=first: ec.matmul(
                            out_ap, lhs, rhs, start=first, stop=False),
                            deps=(war + kdep + qz_done.get(cck, qkv_done)) if first else [])
                        first = False
                        kv_rd[kb].append(t_last)
                    mrep = bass.AP(mb01, 0, [[256, 128], [128, nd], [0, 2], [1, 128]])
                    t_last = P.emit("tensor", lambda ec, scp=scp, width2=width2, mrep=mrep: ec.matmul(
                        scp[:, 0:width2], ident[:, :], mrep, start=False, stop=True),
                        deps=[t_m01, t_ident])
                    qT_rd.append(t_last)
                    if len(pending) >= 3:
                        emit_pv(pending.pop(0))
                    pm = pmask[:, g * 512: g * 512 + width2]
                    t_exp = P.emit("scalar", lambda ec, pm=pm, scp=scp, width2=width2: ec.activation(
                        pm, scp[:, 0:width2], AF.Exp, bias=-8.0, scale=0.125),
                        deps=[t_last] + pmask_rd[g] + t_cst)
                    psc_readers[bank] = [t_exp]
                    pending.append((t_exp, g, i, qg, deltas, nd, hp))
                while pending:
                    emit_pv(pending.pop(0))
                pv_last = pv_last_tile

                if sparse_here:
                    nsl = 2 if t_ttB is not None else 1
                    for bank2 in range(3):
                        lo = (0, 7 * SLOT, 14 * SLOT)[bank2]
                        hi = (7 * SLOT, 14 * SLOT, H * SLOT)[bank2]
                        for sl in range(nsl):
                            rhs = tmpsp[:, sl * VA_W + lo: sl * VA_W + hi]
                            out_ap = po[bank2][:, 0:hi - lo]
                            dep_t = t_ttA if sl == 0 else t_ttB
                            stp = (sl == nsl - 1)
                            t_pvf = P.emit("tensor", lambda ec, out_ap=out_ap, rhs=rhs, stp=stp: ec.matmul(
                                out_ap, ident[:, :], rhs, start=False, stop=stp),
                                deps=[dep_t, t_ident, pv_last[bank2]])
                            pv_last[bank2] = t_pvf
                    tt1_rd[0] = [pv_last[0], pv_last[1], pv_last[2]]
                    if nsl == 2:
                        scr_rd = mx(*(scr_rd + [pv_last[2]]))

                osb_tile = []
                for bank2 in range(3):
                    nh = (7, 7, 2)[bank2]
                    h0 = (0, 7, 14)[bank2]
                    dep_pv = [pv_last[bank2]]
                    dap = bass.AP(po[bank2], 0, [[512, 128], [SLOT, nh]])
                    t_rc = P.emit("vector", lambda ec, h0=h0, nh=nh, dap=dap: ec.reciprocal(
                        recip16[:, h0:h0 + nh], dap), deps=dep_pv)
                    rrep = bass.AP(recip16, h0, [[16, 128], [1, nh], [0, 64]])
                    osrc = bass.AP(po[bank2], 1, [[512, 128], [SLOT, nh], [1, 64]])
                    oout = xln[:, TPC * C + i * C + h0 * 64: TPC * C + i * C + (h0 + nh) * 64]
                    t_oe = P.emit("vector", lambda ec, oout=oout, osrc=osrc, rrep=rrep: ec.tensor_tensor(
                        oout, osrc, rrep, OP.mult), deps=[t_rc] + dep_pv + osb_rd)
                    rrd_set(5 + bank2, [t_oe])
                    osb_tile.append(t_oe)
                    osb_ready.append(t_oe)

                def ot_deps(dc, osb_tile=osb_tile):
                    bks = {po_slot2(2 * dc)[0], po_slot2(2 * dc + 1)[0]}
                    return [osb_tile[k] for k in sorted(bks)]
                oT_ready_i, last_otr = emit_transpose(
                    xln, TPC * C, oT, 0, [None] * TPC,
                    oT_rd, tiles=[i], dep_by_dc=ot_deps)
                osb_rd = [last_otr]
                oT_ready_all += oT_ready_i
                oT_dep = mx(*oT_ready_i)
                for g in range(2):
                    bi = pb_next()
                    pb = pbank[bi]
                    t_mm = None
                    for dc in range(NDC):
                        lhs = oT[:, dc * CH + i * 128: dc * CH + (i + 1) * 128]
                        rhs = wproj[:, dc * C + g * 512: dc * C + (g + 1) * 512]
                        otd = [oT_ready_i[dc]] if len(oT_ready_i) == NDC else oT_dep
                        t_mm = P.emit("tensor", lambda ec, pb=pb, lhs=lhs, rhs=rhs, dc=dc: ec.matmul(
                            pb[:, 0:512], lhs, rhs, start=(dc == 0), stop=False),
                            deps=(pbr_get(bi) + otd + [t_wproj]) if dc == 0 else otd)
                    rhsb = bproj_row[0:32, g * 512:(g + 1) * 512]
                    t_mm = P.emit("tensor", lambda ec, pb=pb, rhsb=rhsb: ec.matmul(
                        pb[:, 0:512], ones32[0:32, :], rhsb, start=False, stop=True),
                        deps=[t_ones, t_bproj])
                    sc0 = 16 * i + 8
                    xap = bass.AP(xbuf, (r * TPC + i) * C + g * 512,
                                  [[2 * TPC * C, 128], [1, 512]])
                    acc = stats[:, sc0 + g: sc0 + g + 1]
                    t_r = P.emit("vector", lambda ec, pb=pb, xap=xap, acc=acc: ec.scalar_tensor_tensor(
                        xap, pb[:, 0:512], 1.0, xap, OP.mult, OP.add, accum_out=acc),
                        deps=[t_mm, t_norm1[i]] + stats_rd[2 * i + 1])
                    pbr_set(bi, [t_r])
                    t_resid1[2 * i + g] = t_r
                    stats_rd[2 * i + 1] = [t_r]
                if i == 0 and idx + 1 < len(sched):
                    ln1_stats_next = emit_ln1_stats(*sched[idx + 1])
                if i == 1 and idx + 1 < len(sched) and ln1_stats_next is not None:
                    ln1_cache = (ln1_stats_next, emit_ln1_tr(ln1_stats_next))
                    xT1_ready[0] = ln1_cache[1]
            oT_rd = mx(*t_resid1)

            pre2 = [[t_resid1[2 * i], t_resid1[2 * i + 1]] for i in range(TPC)]
            t_norm2 = emit_ln(c, 1, pre2, "stats")
            xln_w[1] = [t_norm2[-1]]

            if idx + 1 < len(sched):
                nb, nc2 = sched[idx + 1]
                qkv_st = emit_qkv_qk(nb, nc2)
            else:
                qkv_st = None

            xT2_ready, last_tr2 = emit_transpose(xln, TPC * C, xT, NDC * CH, t_norm2, xT_rd[1])
            xln_rd[1] = [last_tr2]
            osb_rd = [last_tr2]
            xT_rd[1] = []

            if qkv_st is not None:
                qkv_cache = emit_qkv_v(qkv_st)
                nb3, nc3 = sched[idx + 1]
                sp_steps, sp_finish, sparse_cache = build_sparse(
                    nc3, qkv_cache[1], qkv_cache[4])
            else:
                qkv_cache = None
                sp_steps, sp_finish, sparse_cache = [], None, None

            def pop_sp(n):
                for _ in range(n):
                    if sp_steps:
                        sp_steps.pop(0)()

            bi = pb_next()
            pbS = pbank[bi]
            t_mm = None
            for dc in range(NDC):
                lhs = win[:, dc * 64:(dc + 1) * 64]
                rhs = xT[:, NDC * CH + dc * CH:(NDC * CH) + (dc + 1) * CH]
                xtd = [xT2_ready[dc]] if len(xT2_ready) == NDC else xT2_ready
                t_mm = P.emit("tensor", lambda ec, pbS=pbS, lhs=lhs, rhs=rhs, dc=dc: ec.matmul(
                    pbS[0:64, 0:CH], lhs, rhs, start=(dc == 0), stop=(dc == NDC - 1)),
                    deps=(pbr_get(bi) + xtd + [t_win]) if dc == 0 else xtd)
            t_sT = P.emit("scalar", lambda ec, pbS=pbS: ec.activation(
                sTb[:, 0:CH], pbS[0:64, 0:CH], AF.Identity,
                bias=bin_col[:, 0:1], scale=1.0), deps=[t_mm, t_binc] + sTb_rd)
            pbr_set(bi, [t_sT])
            xT_rd[1] = [t_sT]

            biG = pb_next()
            pbG = pbank[biG]
            t_g_last = None
            HPOOL = (2, 3, 4, 7)
            hrot = [0]
            h_banks = {}
            t_hmm = {}

            def emit_hmm(ic, extra, bank=None):
                if bank is None:
                    bank = HPOOL[hrot[0] % 4]
                    hrot[0] += 1
                    if rotb[bank] is pbG:
                        bank = HPOOL[hrot[0] % 4]
                        hrot[0] += 1
                scp = rotb[bank]
                war = psc_readers[bank]
                lhs = w1exp[:, ic * 128:(ic + 1) * 128]
                t_h = P.emit("tensor", lambda ec, scp=scp, lhs=lhs: ec.matmul(
                    scp[:, 0:CH], lhs, sTb[:, 0:CH], start=True, stop=True),
                    deps=[t_sT] + war + extra + ([t_w12[-1]] if ic == 0 else []))
                h_banks[ic] = bank
                t_hmm[ic] = t_h
                return t_h

            NHB = 3
            for ic in range(NHB):
                sTb_rd = [emit_hmm(ic, [])]
                pop_sp(1)
            t_g_last = None
            for ic in range(8):
                bank = h_banks[ic]
                scp = rotb[bank]
                hs = ic % 2
                bap = b1col[:, ic:ic + 1]
                hout = hT[:, hs * CH:(hs + 1) * CH]
                if ic % 2 == 0:
                    t_hev = P.emit("scalar", lambda ec, hout=hout, scp=scp, bap=bap: ec.activation(
                        hout, scp[:, 0:CH], AF.Relu, bias=bap, scale=1.0),
                        deps=[t_hmm[ic], t_b1c] + hT_rd[hs])
                else:
                    t_hev = P.emit("vector", lambda ec, hout=hout, scp=scp, bap=bap: ec.tensor_scalar(
                        hout, scp[:, 0:CH], bap, 0.0, OP.add, OP.max),
                        deps=[t_hmm[ic], t_b1c] + hT_rd[hs])
                psc_readers[bank] = [t_hev]
                if ic + NHB < 8:
                    sTb_rd = [emit_hmm(ic + NHB, [t_hev], bank=bank)]
                lhs2 = w2exp[:, ic * 64:(ic + 1) * 64]
                hin = hT[:, hs * CH:(hs + 1) * CH]
                t_g_last = P.emit("tensor", lambda ec, pbG=pbG, lhs2=lhs2, hin=hin, ic=ic: ec.matmul(
                    pbG[0:64, 0:CH], lhs2, hin, start=(ic == 0), stop=(ic == 7)),
                    deps=[t_hev] + (pbr_get(biG) if ic == 0 else []))
                hT_rd[hs] = [t_g_last]
                pop_sp(2)
            t_gT = P.emit("scalar", lambda ec, pbG=pbG: ec.activation(
                gaugT[0:64, 0:CH], pbG[0:64, 0:CH], AF.Identity,
                bias=b2col[:, 0:1], scale=1.0), deps=[t_g_last, t_b2c, t_gaug1] + gaug_rd)
            pbr_set(biG, [t_gT])
            while sp_steps:
                sp_steps.pop(0)()
            if sp_finish is not None:
                sp_finish()

            t_resid2 = []
            WOB = (3, 4, 5, 6)
            for i in range(TPC):
                for g in range(2):
                    bk2 = WOB[2 * i + g]
                    pb = rotb[bk2]
                    lhs = gaugT[:, i * 128:(i + 1) * 128]
                    rhs = woutaug[:, g * 512:(g + 1) * 512]
                    t_mm = P.emit("tensor", lambda ec, pb=pb, lhs=lhs, rhs=rhs: ec.matmul(
                        pb[:, 0:512], lhs, rhs, start=True, stop=True),
                        deps=[t_gT, t_wo0, t_wo1] + psc_readers[bk2])
                    gaug_rd = [t_mm]
                    xap = bass.AP(xbuf, (r * TPC + i) * C + g * 512,
                                  [[2 * TPC * C, 128], [1, 512]])
                    t_r = P.emit("vector", lambda ec, pb=pb, xap=xap: ec.scalar_tensor_tensor(
                        xap, pb[:, 0:512], 1.0, xap, OP.mult, OP.add),
                        deps=[t_mm, t_resid1[2 * i + g], t_norm2[i]])
                    psc_readers[bk2] = [t_r]
                    t_resid2.append(t_r)

            dstY = d_y[b, t0:t0 + CH, :].rearrange("(n p) c -> p n c", p=128)
            srcY = xbuf[:, r * TPC * C:(r + 1) * TPC * C]
            t_st = P.emit("sync", lambda ec, dstY=dstY, srcY=srcY: ec.dma_start(dstY, srcY),
                          deps=t_resid2, dma=True, dsem="dy%d" % r)
            x_ring_free[r] = [t_st]

        fin = [(k, P.cnt[k]) for k in Prog.DSEMS if P.cnt[k] > 0]
        P.wait_only("sync", fin)

        P.finalize()
        block = st.enter_context(nc.Block())

        @block.tensor
        def _(ec):
            P.replay("tensor", ec, sems)

        @block.scalar
        def _(ec):
            P.replay("scalar", ec, sems)

        @block.vector
        def _(ec):
            P.replay("vector", ec, sems)

        @block.gpsimd
        def _(ec):
            P.replay("gpsimd", ec, sems)

        @block.sync
        def _(ec):
            P.replay("sync", ec, sems)

    return nc


def host_aux(cfg, inputs):
    b_qkv = np.ascontiguousarray(inputs["L0_attn_b_qkv"], dtype=np.float32)
    b1 = np.ascontiguousarray(inputs["L0_ffn_b1"], dtype=np.float32)
    W1 = np.ascontiguousarray(inputs["L0_ffn_W1"], dtype=np.float32)
    W2 = np.ascontiguousarray(inputs["L0_ffn_W2"], dtype=np.float32)
    NCC = 2 * (cfg.C // 128)
    aux = {}
    aux["bqkv_col"] = np.ascontiguousarray(b_qkv[:2 * cfg.C].reshape(NCC, 128).T)
    aux["w1dup"] = np.ascontiguousarray(np.concatenate([W1, W1], axis=0))
    aux["w2dup"] = np.ascontiguousarray(np.concatenate([W2, W2], axis=0))
    b1c = np.empty((128, 8), dtype=np.float32)
    for ic in range(8):
        b1c[0:64, ic] = b1[:, 2 * ic]
        b1c[64:128, ic] = b1[:, 2 * ic + 1]
    aux["b1col"] = b1c
    return aux


def make_in_maps(cfg, inputs):
    a = aux_inputs(cfg)
    aux = host_aux(cfg, inputs)
    x = np.ascontiguousarray(inputs["L0_attn_x"], dtype=np.float32)
    base = {
        "wqkv": np.ascontiguousarray(inputs["L0_attn_W_qkv"], dtype=np.float32),
        "bqkv": np.ascontiguousarray(inputs["L0_attn_b_qkv"], dtype=np.float32),
        "bqkv_col": aux["bqkv_col"],
        "wproj": np.ascontiguousarray(inputs["L0_attn_W_proj"], dtype=np.float32),
        "bproj": np.ascontiguousarray(inputs["L0_attn_b_proj"], dtype=np.float32),
        "win": np.ascontiguousarray(inputs["L0_ffn_Win"], dtype=np.float32),
        "bin": np.ascontiguousarray(inputs["L0_ffn_bin_"], dtype=np.float32),
        "w1dup": aux["w1dup"],
        "b1col": aux["b1col"],
        "w2dup": aux["w2dup"],
        "b2": np.ascontiguousarray(inputs["L0_ffn_b2"], dtype=np.float32),
        "wout": np.ascontiguousarray(inputs["L0_ffn_Wout"], dtype=np.float32),
        "bout": np.ascontiguousarray(inputs["L0_ffn_bout"], dtype=np.float32),
        "m01": a["m01"],
        "mb01x": a["mb01x"],
        "ident": a["ident"],
    }
    in_maps = []
    for core in range(cfg.ncore):
        m = dict(base)
        m["x"] = np.ascontiguousarray(x[core * cfg.BPC:(core + 1) * cfg.BPC])
        in_maps.append(m)
    return in_maps


def kernel(**inputs):
    cfg = CFG()
    nc = build_program(cfg)
    in_maps = make_in_maps(cfg, inputs)
    res = run_bass_kernel_spmd(nc, in_maps, core_ids=list(range(cfg.ncore)))
    out = np.concatenate([r["y"] for r in res.results], axis=0)
    return np.ascontiguousarray(out.astype(np.float32))
